# version 05
# baseline (speedup 1.0000x reference)
import contextlib
import numpy as np

import concourse.bass as bass
import concourse.mybir as mybir
from concourse.bass_utils import run_bass_kernel_spmd

F32 = mybir.dt.float32
AF = mybir.ActivationFunctionType
ALU = mybir.AluOpType
AX = mybir.AxisListType

T = 8192
DM = 1024
DI = 2048
TC1 = 512
NCH1 = T // TC1
TC2 = 1024
NCH2 = T // TC2
QT = T // 4
MG = TC1 + 3
NLOC = 16
HQ = QT + 1
XQR = QT + 5
NPOS = QT + 1

DEBUG = False
STAGES = 5


class K:
    def __init__(self, nc, es):
        self.nc = nc
        self.es = es
        self.sems = {}
        self.n = {}

    def sem(self, name):
        if name not in self.sems:
            self.sems[name] = self.es.enter_context(self.nc.semaphore(name))
            self.n[name] = 0
        return self.sems[name]

    def inc(self, inst, name, amt=1):
        s = self.sem(name)
        self.n[name] += amt
        assert self.n[name] < 64000, f"sem {name} overflow"
        inst.then_inc(s, amt)

    def w(self, eng, name, val=None):
        s = self.sem(name)
        eng.wait_ge(s, self.n[name] if val is None else val)

    def __getitem__(self, name):
        self.sem(name)
        return self.n[name]


S1CH = 16
S1PART = 9
S4PART = 9
S4HALO = 1
S4TILES = 4
S4SUB = 9


def build_program(debug=False, stages=5):
    global S1CH, S1PART, S4PART, S4HALO, S4TILES, S4SUB
    nc = bass.Bass("TRN2", target_bir_lowering=False, debug=False,
                   detect_race_conditions=False)

    x_b = nc.dram_tensor("x_b", [T, DM], F32, kind="ExternalInput").ap()
    x_q = nc.dram_tensor("x_q", [XQR, DM], F32, kind="ExternalInput").ap()
    w_in = nc.dram_tensor("w_in", [DM, 2 * DI], F32, kind="ExternalInput").ap()
    wx_p = nc.dram_tensor("wx_p", [DI, 33], F32, kind="ExternalInput").ap()
    w_out = nc.dram_tensor("w_out", [DI, DM], F32, kind="ExternalInput").ap()
    w_r = nc.dram_tensor("w_r", [DM, DM], F32, kind="ExternalInput").ap()
    w_k = nc.dram_tensor("w_k", [DM, DM], F32, kind="ExternalInput").ap()
    w_v = nc.dram_tensor("w_v", [DM, DM], F32, kind="ExternalInput").ap()
    wmod = nc.dram_tensor("wmod", [DM, 6 * DM], F32, kind="ExternalInput").ap()
    convw_p = nc.dram_tensor("convw_p", [128, 64], F32, kind="ExternalInput").ap()
    convb_p = nc.dram_tensor("convb_p", [128, 16], F32, kind="ExternalInput").ap()
    beta_p = nc.dram_tensor("beta_p", [128, 16], F32, kind="ExternalInput").ap()
    dvec_p = nc.dram_tensor("dvec_p", [128, 16], F32, kind="ExternalInput").ap()
    alog_p = nc.dram_tensor("alog_p", [128, 16 * NLOC], F32, kind="ExternalInput").ap()
    mur_p = nc.dram_tensor("mur_p", [128, 8], F32, kind="ExternalInput").ap()
    muk_p = nc.dram_tensor("muk_p", [128, 8], F32, kind="ExternalInput").ap()
    c_p = nc.dram_tensor("c_p", [128, 8], F32, kind="ExternalInput").ap()
    eye_in = nc.dram_tensor("eye_in", [128, 128], F32, kind="ExternalInput").ap()
    onesr_in = nc.dram_tensor("onesr_in", [1, 128], F32, kind="ExternalInput").ap()
    onesc_in = nc.dram_tensor("onesc_in", [128, 1], F32, kind="ExternalInput").ap()
    halo_in = nc.dram_tensor("halo_in", [128, 1], F32, kind="ExternalInput").ap()

    out_q = nc.dram_tensor("out_q", [QT, DM], F32, kind="ExternalOutput").ap()

    dtbuf = nc.dram_tensor("dtbuf", [128, 16 * T], F32, kind="Internal").ap()
    ubuf = nc.dram_tensor("ubuf", [128, 16 * T], F32, kind="Internal").ap()
    bcbuf = nc.dram_tensor("bcbuf", [128, T], F32, kind="Internal").ap()
    xcq = nc.dram_tensor("xcq", [128, 16 * NPOS], F32, kind="Internal").ap()
    zq = nc.dram_tensor("zq", [128, 16 * NPOS], F32, kind="Internal").ap()
    ystage = nc.dram_tensor("ystage", [4 * HQ, DI], F32, kind="Internal").ap()
    yq = nc.dram_tensor("yq", [HQ, DI], F32, kind="Internal").ap()
    wrT = nc.dram_tensor("wrT", [DM, DM], F32, kind="Internal").ap()
    wkT = nc.dram_tensor("wkT", [DM, DM], F32, kind="Internal").ap()
    wvT = nc.dram_tensor("wvT", [DM, DM], F32, kind="Internal").ap()

    dbg = {}
    if debug:
        for nm, shp in (("mod2", [128, 48]), ("hch", [128, 8 * TC1]),
                        ("xconv", [128, 16 * TC1]), ("dt", [128, 16 * TC1]),
                        ("bc", [128, TC1]), ("ystage", [4 * HQ, DI]),
                        ("yq", [HQ, DI]), ("xcq", [128, 16 * NPOS]),
                        ("zq", [128, 16 * NPOS]), ("xmid", [128, 8 * 513]),
                        ("h2", [128, 8 * 513]),
                        ("dt2w0", [128, 8 * TC2]), ("u2w0", [128, 8 * TC2]),
                        ("dt2w1", [128, 8 * TC2]), ("u2w1", [128, 8 * TC2]),
                        ("bcr2c0", [128, TC2])):
            dbg[nm] = nc.dram_tensor("dbg_" + nm, shp, F32,
                                     kind="ExternalOutput").ap()

    es = contextlib.ExitStack()
    with es:
        e = es.enter_context
        k = K(nc, es)
        sem, inc, W = k.sem, k.inc, k.w
        N = k

        mod2 = e(nc.sbuf_tensor("mod2", [128, 48], F32))
        aneg = e(nc.sbuf_tensor("aneg", [128, 16 * NLOC], F32))
        convw = e(nc.sbuf_tensor("convw", [128, 64], F32))
        convb = e(nc.sbuf_tensor("convb", [128, 16], F32))
        beta = e(nc.sbuf_tensor("beta", [128, 16], F32))
        dvec = e(nc.sbuf_tensor("dvec", [128, 16], F32))
        mur = e(nc.sbuf_tensor("mur", [128, 8], F32))
        muk = e(nc.sbuf_tensor("muk", [128, 8], F32))
        eye = e(nc.sbuf_tensor("eye", [128, 128], F32))
        onesr = e(nc.sbuf_tensor("onesr", [1, 128], F32))
        onesc = e(nc.sbuf_tensor("onesc", [128, 1], F32))
        halosc = e(nc.sbuf_tensor("halosc", [128, 1], F32))
        states = e(nc.sbuf_tensor("states", [128, 16 * NLOC], F32))
        wxsb = e(nc.sbuf_tensor("wxsb", [128, 16 * 33], F32))
        epsv = e(nc.sbuf_tensor("epsv", [128, 1], F32))
        onep = e(nc.sbuf_tensor("onep", [128, 1], F32))

        ps = [e(nc.psum_tensor(f"ps{i}", [128, 512], F32)) for i in range(8)]

        block = e(nc.Block())
        SA, CA1, GA, SB, CB1, GB = 0, 1, 2, 3, 4, 5

        @block.sync
        def _(s):
            for dst, src in ((convw, convw_p), (convb, convb_p), (beta, beta_p),
                             (dvec, dvec_p), (mur, mur_p), (muk, muk_p),
                             (eye, eye_in), (onesr, onesr_in),
                             (onesc, onesc_in),
                             (halosc, halo_in), (aneg, alog_p)):
                inc(s.dma_start(out=dst[:, :], in_=src[:, :]), "ld0", 16)
            for kk in range(16):
                inc(s.dma_start(out=wxsb[:, kk * 33:(kk + 1) * 33],
                                in_=wx_p[kk * 128:(kk + 1) * 128, :]), "ld0", 16)

        @block.vector
        def _(v):
            W(v, "ld0")
            v.memset(states[:, :], 0.0)
            v.memset(onep[:, :], 1.0)
            inc(v.memset(epsv[:, :], 1e-5), "init", 1)

        @block.scalar
        def _(a):
            W(a, "ld0")
            inc(a.activation(out=aneg[:, :], in_=aneg[:, :], func=AF.Exp),
                "anege", 1)
            W(a, "anege")
            inc(a.activation(out=aneg[:, :], in_=aneg[:, :], func=AF.Copy,
                             scale=-1.0), "anegd", 1)

        if stages == 11:
            @block.sync
            def _(s):
                W(s, "ld0")
                W(s, "init")
            @block.scalar
            def _(a):
                W(a, "anegd")
            return nc

        with nc.sbuf_tensor("cs", [128, 8], F32) as cs, \
             nc.sbuf_tensor("wmt", [128, 128], F32) as wmt, \
             nc.sbuf_tensor("modraw", [128, 48], F32) as modraw, \
             nc.sbuf_tensor("gtmp", [128, 16], F32) as gtmp:

            @block.sync
            def _(s):
                inc(s.dma_start(out=cs[:, :], in_=c_p[:, :]), "ldc", 16)

            @block.scalar
            def _(a):
                W(a, "ldc")
                inc(a.activation(out=cs[:, :], in_=cs[:, :], func=AF.Silu),
                    "csd", 1)

            @block.sync
            def _(s):
                for oc in range(48):
                    for kk in range(8):
                        i2 = oc * 8 + kk
                        if i2 >= 1:
                            W(s, "modmm", i2)
                        inc(s.dma_start(out=wmt[:, :],
                                        in_=wmod[kk * 128:(kk + 1) * 128,
                                                 oc * 128:(oc + 1) * 128]),
                            "ldwm", 16)

            @block.tensor
            def _(t):
                W(t, "csd")
                W(t, "ld0")
                for oc in range(48):
                    for kk in range(8):
                        t.wait_ge(sem("ldwm"), (oc * 8 + kk + 1) * 16)
                        if kk == 0 and oc > 0:
                            t.wait_ge(sem("modev"), oc)
                        inc(t.matmul(ps[0][:, 0:1], wmt[:, :], cs[:, kk:kk + 1],
                                     start=(kk == 0), stop=(kk == 7)), "modmm", 1)

            @block.scalar
            def _(a):
                for oc in range(48):
                    a.wait_ge(sem("modmm"), (oc + 1) * 8)
                    inc(a.activation(out=modraw[:, oc:oc + 1], in_=ps[0][:, 0:1],
                                     func=AF.Copy), "modev", 1)

            @block.vector
            def _(v):
                W(v, "modev")
                v.tensor_copy(out=mod2[:, SA * 8:SA * 8 + 8], in_=modraw[:, 0:8])
                v.tensor_scalar_add(out=mod2[:, CA1 * 8:CA1 * 8 + 8],
                                    in0=modraw[:, 8:16], scalar1=1.0)
                v.tensor_copy(out=mod2[:, SB * 8:SB * 8 + 8], in_=modraw[:, 24:32])
                v.tensor_scalar_add(out=mod2[:, CB1 * 8:CB1 * 8 + 8],
                                    in0=modraw[:, 32:40], scalar1=1.0)
                v.tensor_single_scalar(out=gtmp[:, 0:8], in_=modraw[:, 16:24],
                                       scalar=0.0, op=ALU.is_ge)
                v.tensor_single_scalar(out=gtmp[:, 8:16], in_=modraw[:, 40:48],
                                       scalar=0.0, op=ALU.is_ge)
                v.tensor_scalar(out=gtmp[:, 0:8], in0=gtmp[:, 0:8], scalar1=2.0,
                                scalar2=-1.0, op0=ALU.mult, op1=ALU.add)
                inc(v.tensor_scalar(out=gtmp[:, 8:16], in0=gtmp[:, 8:16],
                                    scalar1=2.0, scalar2=-1.0, op0=ALU.mult,
                                    op1=ALU.add), "sgn", 1)

            @block.scalar
            def _(a):
                W(a, "sgn")
                a.activation(out=modraw[:, 16:24], in_=modraw[:, 16:24],
                             func=AF.Abs)
                inc(a.activation(out=modraw[:, 40:48], in_=modraw[:, 40:48],
                                 func=AF.Abs), "absd", 1)

            @block.vector
            def _(v):
                W(v, "absd")
                v.tensor_scalar_max(out=modraw[:, 16:24], in0=modraw[:, 16:24],
                                    scalar1=0.25)
                v.tensor_scalar_max(out=modraw[:, 40:48], in0=modraw[:, 40:48],
                                    scalar1=0.25)
                v.tensor_mul(out=mod2[:, GA * 8:GA * 8 + 8],
                             in0=modraw[:, 16:24], in1=gtmp[:, 0:8])
                inc(v.tensor_mul(out=mod2[:, GB * 8:GB * 8 + 8],
                                 in0=modraw[:, 40:48], in1=gtmp[:, 8:16]),
                    "modok", 1)

            if stages == 12:
                @block.sync
                def _(s):
                    W(s, "modok")
                return nc

            @block.sync
            def _(s):
                W(s, "modev")
                for wi, (wsrc, wdst) in enumerate(((w_r, wrT), (w_k, wkT),
                                                  (w_v, wvT))):
                    for jj in range(8):
                        for cci in range(8):
                            idx = wi * 64 + jj * 8 + cci
                            if idx >= 1:
                                s.wait_ge(sem("wtst"), 16 * idx)
                            inc(s.dma_start(
                                out=wmt[:, :],
                                in_=wsrc[jj * 128:(jj + 1) * 128,
                                         cci * 128:(cci + 1) * 128]), "wtld", 16)
                            s.wait_ge(sem("wtev"), idx + 1)
                            inc(s.dma_start(
                                out=wdst[cci * 128:(cci + 1) * 128,
                                         jj * 128:(jj + 1) * 128],
                                in_=wmt[:, :]), "wtst", 16)

            @block.tensor
            def _(t):
                for idx in range(192):
                    t.wait_ge(sem("wtld"), 16 * (idx + 1))
                    if idx > 0:
                        t.wait_ge(sem("wtev"), idx)
                    inc(t.transpose(ps[1][:, 0:128], wmt[:, :], eye[:, :]),
                        "wttr", 1)

            @block.scalar
            def _(a):
                for idx in range(192):
                    a.wait_ge(sem("wttr"), idx + 1)
                    inc(a.activation(out=wmt[:, :], in_=ps[1][:, 0:128],
                                     func=AF.Copy), "wtev", 1)


        if stages == 13:
            @block.sync
            def _(s):
                W(s, "wtst")
                W(s, "modok")
            return nc

        def phase1(tag, xsrc, nrows, nchunks, widths, store):
            pass

        with nc.sbuf_tensor("xt", [128, 4 * DM], F32) as xt, \
             nc.sbuf_tensor("x2t", [128, DM], F32) as x2t, \
             nc.sbuf_tensor("stt", [128, 16], F32) as stt, \
             nc.sbuf_tensor("hch", [128, 8 * TC1], F32) as hch, \
             nc.sbuf_tensor("wbuf", [128, 2 * 8 * 128], F32) as wbuf, \
             nc.sbuf_tensor("xssm", [128, 16 * MG], F32) as xssm, \
             nc.sbuf_tensor("xconv", [128, 16 * TC1], F32) as xconv, \
             nc.sbuf_tensor("cacc", [128, 4 * TC1], F32) as cacc, \
             nc.sbuf_tensor("dt_c", [128, 16 * TC1], F32) as dt_c, \
             nc.sbuf_tensor("u_c", [128, 16 * TC1], F32) as u_c, \
             nc.sbuf_tensor("bcrep", [128, TC1], F32) as bcrep, \
             nc.sbuf_tensor("bc32", [32, TC1], F32) as bc32, \
             nc.sbuf_tensor("s_sb", [1, TC1], F32) as s_sb:

            @block.vector
            def _(v):
                for blk in range(16):
                    v.memset(xssm[:, blk * MG:blk * MG + 3], 0.0)

            for c in range(min(S1CH, NCH1)):
                @block.sync
                def _(s, c=c):
                    if c > 0:
                        s.wait_ge(sem("htr"), 32 * c)
                    for tt in range(4):
                        inc(s.dma_start(
                            out=xt[:, tt * DM:(tt + 1) * DM],
                            in_=x_b[c * TC1 + tt * 128:
                                    c * TC1 + (tt + 1) * 128, :]), "ldx", 16)

                for tt in range(4):
                    @block.scalar
                    def _(a, c=c, tt=tt):
                        a.wait_ge(sem("ldx"), (4 * c + tt + 1) * 16)
                        if 4 * c + tt > 0:
                            a.wait_ge(sem("vvar"), 4 * c + tt)
                        inc(a.activation(out=x2t[:, :],
                                         in_=xt[:, tt * DM:(tt + 1) * DM],
                                         func=AF.Square), "ax2", 1)

                    @block.vector
                    def _(v, c=c, tt=tt):
                        o = tt * 4
                        xv = xt[:, tt * DM:(tt + 1) * DM]
                        v.wait_ge(sem("ldx"), (4 * c + tt + 1) * 16)
                        v.reduce_sum(out=stt[:, o:o + 1], in_=xv, axis=AX.X)
                        v.wait_ge(sem("ax2"), 4 * c + tt + 1)
                        v.reduce_sum(out=stt[:, o + 2:o + 3], in_=x2t[:, :],
                                     axis=AX.X)
                        v.tensor_scalar_mul(out=stt[:, o + 1:o + 2],
                                            in0=stt[:, o:o + 1],
                                            scalar1=-1.0 / DM)
                        v.tensor_scalar_mul(out=stt[:, o:o + 1],
                                            in0=stt[:, o + 2:o + 3],
                                            scalar1=1.0 / DM)
                        v.tensor_mul(out=stt[:, o + 2:o + 3],
                                     in0=stt[:, o + 1:o + 2],
                                     in1=stt[:, o + 1:o + 2])
                        v.tensor_scalar_mul(out=stt[:, o + 3:o + 4],
                                            in0=stt[:, o + 1:o + 2], scalar1=1.0)
                        inc(v.tensor_sub(out=stt[:, o + 2:o + 3],
                                         in0=stt[:, o:o + 1],
                                         in1=stt[:, o + 2:o + 3]), "vvar", 1)

                    @block.scalar
                    def _(a, c=c, tt=tt):
                        o = tt * 4
                        if c == 0 and tt == 0:
                            W(a, "init")
                        a.wait_ge(sem("vvar"), 4 * c + tt + 1)
                        inc(a.activation(out=stt[:, o + 3:o + 4],
                                         in_=stt[:, o + 2:o + 3],
                                         func=AF.Sqrt, bias=epsv[:, 0:1]), "arstd", 1)

                    @block.vector
                    def _(v, c=c, tt=tt):
                        o = tt * 4
                        xv = xt[:, tt * DM:(tt + 1) * DM]
                        v.wait_ge(sem("arstd"), 4 * c + tt + 1)
                        inc(v.reciprocal(out=stt[:, o + 3:o + 4],
                                         in_=stt[:, o + 3:o + 4]), "vrc", 1)
                        v.wait_ge(sem("vrc"), N["vrc"])
                        v.tensor_scalar_add(out=xv, in0=xv,
                                            scalar1=stt[:, o + 1:o + 2])
                        inc(v.tensor_scalar_mul(out=xv, in0=xv,
                                                scalar1=stt[:, o + 3:o + 4]),
                            "vxh", 1)

                if S1PART < 2:
                    continue
                @block.tensor
                def _(t, c=c):
                    if c == 0:
                        W(t, "modok")
                    for tt in range(4):
                        t.wait_ge(sem("vxh"), 4 * c + tt + 1)
                        for jj in range(8):
                            idx = tt * 8 + jj
                            g = 32 * c + idx
                            if g >= 2:
                                t.wait_ge(sem("hev"), g - 1)
                            bank = ps[2 + (idx % 2)]
                            inc(t.transpose(
                                bank[:, 0:128],
                                xt[:, tt * DM + jj * 128:tt * DM + (jj + 1) * 128],
                                eye[:, :]), "htr", 1)

                @block.scalar
                def _(a, c=c):
                    if c > 0:
                        a.wait_ge(sem("wmm"), 16 * c)
                    for tt in range(4):
                        for jj in range(8):
                            idx = tt * 8 + jj
                            a.wait_ge(sem("htr"), 32 * c + idx + 1)
                            bank = ps[2 + (idx % 2)]
                            inc(a.activation(
                                out=hch[:, jj * TC1 + tt * 128:
                                        jj * TC1 + (tt + 1) * 128],
                                in_=bank[:, 0:128],
                                func=AF.Identity,
                                scale=mod2[:, CA1 * 8 + jj:CA1 * 8 + jj + 1],
                                bias=mod2[:, SA * 8 + jj:SA * 8 + jj + 1]),
                                "hev", 1)

                if S1PART < 3:
                    continue
                @block.sync
                def _(s, c=c):
                    for ji in range(16):
                        g = 16 * c + ji
                        if g >= 2:
                            s.wait_ge(sem("wmm"), g - 1)
                        half = ji % 2
                        for kk in range(8):
                            inc(s.dma_start(
                                out=wbuf[:, half * 1024 + kk * 128:
                                         half * 1024 + (kk + 1) * 128],
                                in_=w_in[kk * 128:(kk + 1) * 128,
                                         ji * 128:(ji + 1) * 128]), "ldw", 16)

                @block.tensor
                def _(t, c=c):
                    t.wait_ge(sem("hev"), 32 * (c + 1))
                    for ji in range(16):
                        g = 16 * c + ji
                        half = ji % 2
                        t.wait_ge(sem("ldw"), (g + 1) * 8 * 16)
                        if g >= 2:
                            t.wait_ge(sem("psev"), g - 1)
                        bank = ps[4] if half == 0 else ps[5]
                        for kk in range(8):
                            mi = t.matmul(bank[:, :],
                                          wbuf[:, half * 1024 + kk * 128:
                                               half * 1024 + (kk + 1) * 128],
                                          hch[:, kk * TC1:(kk + 1) * TC1],
                                          start=(kk == 0), stop=(kk == 7))
                        inc(mi, "wmm", 1)

                @block.scalar
                def _(a, c=c):
                    if c > 0:
                        a.wait_ge(sem("vmarg"), c)
                    for ji in range(16):
                        g = 16 * c + ji
                        half = ji % 2
                        bank = ps[4] if half == 0 else ps[5]
                        a.wait_ge(sem("wmm"), g + 1)
                        inc(a.activation(out=xssm[:, ji * MG + 3:ji * MG + 3 + TC1],
                                         in_=bank[:, :], func=AF.Copy), "psev", 1)

                if S1PART < 4:
                    continue
                @block.vector
                def _(v, c=c):
                    v.wait_ge(sem("psev"), 16 * (c + 1))
                    for blk in range(16):
                        base = blk * MG
                        pr = (blk % 2) * 2 * TC1
                        if 16 * c + blk >= 2:
                            v.wait_ge(sem("acv"), 16 * c + blk - 1)
                        v.tensor_scalar_mul(out=cacc[:, pr:pr + TC1],
                                            in0=xssm[:, base:base + TC1],
                                            scalar1=convw[:, blk * 4:blk * 4 + 1])
                        for kk in range(1, 4):
                            so = pr + ((kk + 1) % 2) * TC1
                            do = pr + (kk % 2) * TC1
                            v.scalar_tensor_tensor(
                                out=cacc[:, do:do + TC1],
                                in0=xssm[:, base + kk:base + kk + TC1],
                                scalar=convw[:, blk * 4 + kk:blk * 4 + kk + 1],
                                in1=cacc[:, so:so + TC1],
                                op0=ALU.mult, op1=ALU.add)
                        inc(v.engine_nop(), "vcv", 1)
                    for blk in range(16):
                        base = blk * MG
                        v.tensor_copy(out=xssm[:, base:base + 3],
                                      in_=xssm[:, base + TC1:base + TC1 + 3])
                    inc(v.engine_nop(), "vmarg", 1)

                @block.scalar
                def _(a, c=c):
                    if c > 0:
                        a.wait_ge(sem("vu"), c)
                    for blk in range(16):
                        pr = (blk % 2) * 2 * TC1
                        a.wait_ge(sem("vcv"), 16 * c + blk + 1)
                        inc(a.activation(out=xconv[:, blk * TC1:(blk + 1) * TC1],
                                         in_=cacc[:, pr + TC1:pr + 2 * TC1],
                                         func=AF.Silu,
                                         bias=convb[:, blk:blk + 1]), "acv", 1)

                if S1PART < 5:
                    continue
                @block.tensor
                def _(t, c=c):
                    t.wait_ge(sem("acv"), 16 * (c + 1))
                    if c > 0:
                        t.wait_ge(sem("bcrp"), 64 * c)
                        t.wait_ge(sem("asev"), c)
                    for kk in range(16):
                        mi = t.matmul(ps[6][0:33, :],
                                      wxsb[:, kk * 33:(kk + 1) * 33],
                                      xconv[:, kk * TC1:(kk + 1) * TC1],
                                      start=(kk == 0), stop=(kk == 15))
                    inc(mi, "wxmm", 1)
                    t.wait_ge(sem("asev"), c + 1)
                    if c > 0:
                        t.wait_ge(sem("adt"), 16 * c)
                    inc(t.matmul(ps[7][:, :], onesr[0:1, :], s_sb[0:1, :],
                                 start=True, stop=True), "sbc", 1)

                @block.scalar
                def _(a, c=c):
                    a.wait_ge(sem("wxmm"), c + 1)
                    if c > 0:
                        a.wait_ge(sem("bcrp"), 64 * c)
                    inc(a.activation(out=bc32[:, :], in_=ps[6][0:32, :],
                                     func=AF.Copy), "bcev", 1)
                    inc(a.activation(out=s_sb[0:1, :], in_=ps[6][32:33, :],
                                     func=AF.Copy), "asev", 1)
                    a.wait_ge(sem("sbc"), c + 1)
                    if c > 0:
                        a.wait_ge(sem("sp1"), 48 * c)
                        a.wait_ge(sem("vu"), c)
                    for blk in range(16):
                        mi = a.activation(out=dt_c[:, blk * TC1:(blk + 1) * TC1],
                                          in_=ps[7][:, :], func=AF.Abs,
                                          bias=beta[:, blk:blk + 1])
                    inc(mi, "dtp1", 1)
                    W(a, "dtp1")
                    for blk in range(16):
                        mi = a.activation(out=dt_c[:, blk * TC1:(blk + 1) * TC1],
                                          in_=dt_c[:, blk * TC1:(blk + 1) * TC1],
                                          func=AF.Exp, scale=-1.0)
                    inc(mi, "dtp2", 1)
                    W(a, "dtp2")
                    for blk in range(16):
                        mi = a.activation(out=dt_c[:, blk * TC1:(blk + 1) * TC1],
                                          in_=dt_c[:, blk * TC1:(blk + 1) * TC1],
                                          func=AF.Ln, bias=onep[:, 0:1])
                    inc(mi, "dtp3", 1)
                    W(a, "dtp3")
                    for blk in range(16):
                        inc(a.activation(out=u_c[:, blk * TC1:(blk + 1) * TC1],
                                         in_=ps[7][:, :], func=AF.Relu,
                                         bias=beta[:, blk:blk + 1]), "adt", 1)

                @block.sync
                def _(s, c=c):
                    s.wait_ge(sem("bcev"), c + 1)
                    if c > 0:
                        s.wait_ge(sem("sp1"), 48 * c)
                    for qd in range(4):
                        inc(s.dma_start(out=bcrep[qd * 32:(qd + 1) * 32, :],
                                        in_=bc32[:, :]), "bcrp", 16)

                if S1PART < 6:
                    continue
                @block.vector
                def _(v, c=c):
                    v.wait_ge(sem("adt"), 16 * (c + 1))
                    if c > 0:
                        v.wait_ge(sem("sp1"), 48 * c)
                    for blk in range(16):
                        v.tensor_add(out=dt_c[:, blk * TC1:(blk + 1) * TC1],
                                     in0=dt_c[:, blk * TC1:(blk + 1) * TC1],
                                     in1=u_c[:, blk * TC1:(blk + 1) * TC1])
                    for blk in range(16):
                        v.tensor_mul(out=u_c[:, blk * TC1:(blk + 1) * TC1],
                                     in0=dt_c[:, blk * TC1:(blk + 1) * TC1],
                                     in1=xconv[:, blk * TC1:(blk + 1) * TC1])
                    inc(v.engine_nop(), "vu", 1)

                @block.sync
                def _(s, c=c):
                    s.wait_ge(sem("vu"), c + 1)
                    s.wait_ge(sem("bcrp"), 64 * (c + 1))
                    inc(s.dma_start(
                        out=dtbuf[:, :].rearrange("p (b t) -> p b t", b=16)
                        [:, :, c * TC1:(c + 1) * TC1],
                        in_=dt_c[:, :].rearrange("p (b t) -> p b t", b=16)),
                        "sp1", 16)
                    inc(s.dma_start(
                        out=ubuf[:, :].rearrange("p (b t) -> p b t", b=16)
                        [:, :, c * TC1:(c + 1) * TC1],
                        in_=u_c[:, :].rearrange("p (b t) -> p b t", b=16)),
                        "sp1", 16)
                    inc(s.dma_start(out=bcbuf[:, c * TC1:(c + 1) * TC1],
                                    in_=bcrep[:, :]), "sp1", 16)
                    if debug and c == 1:
                        inc(s.dma_start(out=dbg["mod2"], in_=mod2[:, :]),
                            "dbgX", 16)
                        inc(s.dma_start(out=dbg["hch"], in_=hch[:, :]),
                            "dbgX", 16)
                        inc(s.dma_start(out=dbg["xconv"], in_=xconv[:, :]),
                            "dbgX", 16)
                        inc(s.dma_start(out=dbg["dt"], in_=dt_c[:, :]), "dbgX", 16)
                        inc(s.dma_start(out=dbg["bc"], in_=bcrep[:, :]),
                            "dbgX", 16)

        if stages < 2:
            @block.sync
            def _(s):
                nch = min(S1CH, NCH1)
                if S1PART >= 6:
                    s.wait_ge(sem("sp1"), 48 * nch)
                elif S1PART >= 5:
                    s.wait_ge(sem("adt"), 16 * nch)
                    s.wait_ge(sem("bcrp"), 64 * nch)
                elif S1PART >= 4:
                    s.wait_ge(sem("acv"), 16 * nch)
                elif S1PART >= 3:
                    s.wait_ge(sem("psev"), 16 * nch)
                elif S1PART >= 2:
                    s.wait_ge(sem("hev"), 32 * nch)
                else:
                    s.wait_ge(sem("vxh"), 4 * nch)
                if debug and "dbgX" in k.n:
                    W(s, "dbgX")
            return nc
        CH4 = [(0, 512), (512, 512), (1024, 512), (1536, 512), (2048, 5)]
        with nc.sbuf_tensor("xt4", [128, 4 * DM], F32) as xt, \
             nc.sbuf_tensor("x2t4", [128, DM], F32) as x2t, \
             nc.sbuf_tensor("stt4", [128, 16], F32) as stt, \
             nc.sbuf_tensor("hch4", [128, 8 * TC1], F32) as hch, \
             nc.sbuf_tensor("wbuf4a", [128, 2 * 8 * 128], F32) as wbuf, \
             nc.sbuf_tensor("xssm4", [128, 16 * MG], F32) as xssm, \
             nc.sbuf_tensor("xconv4", [128, 16 * TC1], F32) as xconv, \
             nc.sbuf_tensor("cacc4", [128, 4 * TC1], F32) as cacc:

            @block.vector
            def _(v):
                v.wait_ge(sem("sp1"), 48 * NCH1)
                for blk in range(16):
                    v.memset(xssm[:, blk * MG:blk * MG + 3], 0.0)

            for ci, (r0, wdt) in enumerate(CH4):
                ntt = -(-wdt // 128)
                @block.sync
                def _(s, ci=ci, r0=r0, wdt=wdt, ntt=ntt):
                    if ci == 0:
                        s.wait_ge(sem("sp1"), 48 * NCH1)
                    if ci > 0:
                        s.wait_ge(sem("htr4"), N["htr4"])
                    for tt in range(ntt):
                        nr = min(128, wdt - tt * 128)
                        inc(s.dma_start(out=xt[0:nr, tt * DM:(tt + 1) * DM],
                                        in_=x_q[r0 + tt * 128:r0 + tt * 128 + nr, :]),
                            "ldx4", 16)

                for tt in range(ntt):
                    nr = min(128, wdt - tt * 128)

                    @block.scalar
                    def _(a, ci=ci, tt=tt, nr=nr):
                        a.wait_ge(sem("ldx4"), N["ldx4"] - 16 * (ntt - tt - 1))
                        if N["ax24"] > 0:
                            a.wait_ge(sem("vvar4"), N["ax24"])
                        inc(a.activation(out=x2t[0:nr, :],
                                         in_=xt[0:nr, tt * DM:(tt + 1) * DM],
                                         func=AF.Square), "ax24", 1)

                    @block.vector
                    def _(v, ci=ci, tt=tt, nr=nr):
                        o = tt * 4
                        xv = xt[0:nr, tt * DM:(tt + 1) * DM]
                        W(v, "ax24")
                        v.reduce_sum(out=stt[0:nr, o:o + 1], in_=xv, axis=AX.X)
                        v.reduce_sum(out=stt[0:nr, o + 2:o + 3], in_=x2t[0:nr, :],
                                     axis=AX.X)
                        v.tensor_scalar_mul(out=stt[0:nr, o + 1:o + 2],
                                            in0=stt[0:nr, o:o + 1],
                                            scalar1=-1.0 / DM)
                        v.tensor_scalar_mul(out=stt[0:nr, o:o + 1],
                                            in0=stt[0:nr, o + 2:o + 3],
                                            scalar1=1.0 / DM)
                        v.tensor_mul(out=stt[0:nr, o + 2:o + 3],
                                     in0=stt[0:nr, o + 1:o + 2],
                                     in1=stt[0:nr, o + 1:o + 2])
                        v.tensor_scalar_mul(out=stt[0:nr, o + 3:o + 4],
                                            in0=stt[0:nr, o + 1:o + 2],
                                            scalar1=1.0)
                        inc(v.tensor_sub(out=stt[0:nr, o + 2:o + 3],
                                         in0=stt[0:nr, o:o + 1],
                                         in1=stt[0:nr, o + 2:o + 3]), "vvar4", 1)

                    @block.scalar
                    def _(a, tt=tt, nr=nr):
                        o = tt * 4
                        W(a, "vvar4")
                        inc(a.activation(out=stt[0:nr, o + 3:o + 4],
                                         in_=stt[0:nr, o + 2:o + 3],
                                         func=AF.Sqrt, bias=epsv[0:nr, 0:1]), "arstd4", 1)

                    @block.vector
                    def _(v, tt=tt, nr=nr):
                        o = tt * 4
                        xv = xt[0:nr, tt * DM:(tt + 1) * DM]
                        W(v, "arstd4")
                        inc(v.reciprocal(out=stt[0:nr, o + 3:o + 4],
                                         in_=stt[0:nr, o + 3:o + 4]), "vrc4", 1)
                        v.wait_ge(sem("vrc4"), N["vrc4"])
                        v.tensor_scalar_add(out=xv, in0=xv,
                                            scalar1=stt[0:nr, o + 1:o + 2])
                        inc(v.tensor_scalar_mul(out=xv, in0=xv,
                                                scalar1=stt[0:nr, o + 3:o + 4]),
                            "vxh4", 1)

                @block.tensor
                def _(t, ci=ci, wdt=wdt, ntt=ntt):
                    W(t, "vxh4")
                    for tt in range(ntt):
                        nr = min(128, wdt - tt * 128)
                        for jj in range(8):
                            idx = tt * 8 + jj
                            if N["htr4"] >= 2:
                                t.wait_ge(sem("hev4"), N["htr4"] - 1)
                            bank = ps[2 + (N["htr4"] % 2)]
                            inc(t.transpose(
                                bank[:, 0:nr],
                                xt[0:nr, tt * DM + jj * 128:tt * DM + (jj + 1) * 128],
                                eye[0:nr, 0:nr]), "htr4", 1)

                @block.scalar
                def _(a, ci=ci, wdt=wdt, ntt=ntt):
                    if ci > 0:
                        a.wait_ge(sem("wmm4"), N["wmm4"])
                    base = N["htr4"] - 8 * ntt
                    for tt in range(ntt):
                        nr = min(128, wdt - tt * 128)
                        for jj in range(8):
                            idx = tt * 8 + jj
                            a.wait_ge(sem("htr4"), base + idx + 1)
                            bank = ps[2 + ((base + idx) % 2)]
                            inc(a.activation(
                                out=hch[:, jj * TC1 + tt * 128:
                                        jj * TC1 + tt * 128 + nr],
                                in_=bank[:, 0:nr],
                                func=AF.Identity,
                                scale=mod2[:, CA1 * 8 + jj:CA1 * 8 + jj + 1],
                                bias=mod2[:, SA * 8 + jj:SA * 8 + jj + 1]),
                                "hev4", 1)

                @block.sync
                def _(s, ci=ci):
                    for ji in range(32):
                        g = 32 * ci + ji
                        if g >= 2:
                            s.wait_ge(sem("wmm4"), g - 1)
                        half = ji % 2
                        col = ji * 128 if ji < 16 else DI + (ji - 16) * 128
                        for kk in range(8):
                            inc(s.dma_start(
                                out=wbuf[:, half * 1024 + kk * 128:
                                         half * 1024 + (kk + 1) * 128],
                                in_=w_in[kk * 128:(kk + 1) * 128, col:col + 128]),
                                "ldw4", 16)

                @block.tensor
                def _(t, ci=ci, wdt=wdt):
                    W(t, "hev4")
                    for ji in range(32):
                        g = 32 * ci + ji
                        half = ji % 2
                        t.wait_ge(sem("ldw4"), (g + 1) * 8 * 16)
                        if g >= 2:
                            t.wait_ge(sem("psev4"), g - 1)
                        bank = ps[4] if half == 0 else ps[5]
                        for kk in range(8):
                            mi = t.matmul(bank[:, 0:wdt],
                                          wbuf[:, half * 1024 + kk * 128:
                                               half * 1024 + (kk + 1) * 128],
                                          hch[:, kk * TC1:kk * TC1 + wdt],
                                          start=(kk == 0), stop=(kk == 7))
                        inc(mi, "wmm4", 1)

                @block.scalar
                def _(a, ci=ci, r0=r0, wdt=wdt):
                    if ci > 0:
                        a.wait_ge(sem("vmarg4"), ci)
                    base = 32 * ci
                    for ji in range(32):
                        half = ji % 2
                        bank = ps[4] if half == 0 else ps[5]
                        a.wait_ge(sem("wmm4"), base + ji + 1)
                        if ji < 16:
                            inc(a.activation(
                                out=xssm[:, ji * MG + 3:ji * MG + 3 + wdt],
                                in_=bank[:, 0:wdt], func=AF.Copy), "psev4", 1)
                        else:
                            blk = ji - 16
                            if ji == 16:
                                a.wait_ge(sem("xcst4"), N["xcst4"])
                            inc(a.activation(
                                out=xconv[:, blk * TC1:blk * TC1 + wdt],
                                in_=bank[:, 0:wdt], func=AF.Silu), "psev4", 1)

                @block.sync
                def _(s, ci=ci, r0=r0, wdt=wdt):
                    W(s, "psev4")
                    lo = max(0, 4 - r0)
                    npos = wdt - lo
                    if npos > 0:
                        p0 = r0 - 4 + lo
                        inc(s.dma_start(
                            out=zq[:, :].rearrange("p (b t) -> p b t", b=16)
                            [:, :, p0:p0 + npos],
                            in_=xconv[:, :].rearrange("p (b t) -> p b t", b=16)
                            [:, :, lo:lo + npos]), "zst4", 16)

                if ci == 0:
                    @block.vector
                    def _(v):
                        W(v, "psev4")
                        v.tensor_scalar_mul(
                            out=xssm[:, :].rearrange("p (b t) -> p b t", b=16)
                            [:, :, 3:8],
                            in0=xssm[:, :].rearrange("p (b t) -> p b t", b=16)
                            [:, :, 3:8],
                            scalar1=halosc[:, 0:1])
                        v.tensor_scalar_mul(out=cacc[:, 0:1], in0=cacc[:, 0:1],
                                            scalar1=1.0)

                @block.vector
                def _(v, ci=ci, r0=r0, wdt=wdt):

                    v.wait_ge(sem("zst4"), N["zst4"])
                    v.wait_ge(sem("psev4"), N["psev4"])
                    for blk in range(16):
                        base = blk * MG
                        pr = (blk % 2) * 2 * TC1
                        if N["vcv4"] >= 2:
                            v.wait_ge(sem("acv4"), N["vcv4"] - 1)
                        v.tensor_scalar_mul(out=cacc[:, pr:pr + wdt],
                                            in0=xssm[:, base:base + wdt],
                                            scalar1=convw[:, blk * 4:blk * 4 + 1])
                        for kk in range(1, 4):
                            so = pr + ((kk + 1) % 2) * TC1
                            do = pr + (kk % 2) * TC1
                            v.scalar_tensor_tensor(
                                out=cacc[:, do:do + wdt],
                                in0=xssm[:, base + kk:base + kk + wdt],
                                scalar=convw[:, blk * 4 + kk:blk * 4 + kk + 1],
                                in1=cacc[:, so:so + wdt],
                                op0=ALU.mult, op1=ALU.add)
                        inc(v.engine_nop(), "vcv4", 1)
                    for blk in range(16):
                        base = blk * MG
                        v.tensor_copy(out=xssm[:, base:base + 3],
                                      in_=xssm[:, base + wdt:base + wdt + 3])
                    inc(v.engine_nop(), "vmarg4", 1)

                @block.scalar
                def _(a, ci=ci, wdt=wdt):
                    base = N["acv4"]
                    for blk in range(16):
                        pr = (blk % 2) * 2 * TC1
                        a.wait_ge(sem("vcv4"), base + blk + 1)
                        if blk == 0:
                            a.wait_ge(sem("xcst4"), N["xcst4"])
                        inc(a.activation(out=xconv[:, blk * TC1:blk * TC1 + wdt],
                                         in_=cacc[:, pr + TC1:pr + TC1 + wdt],
                                         func=AF.Silu,
                                         bias=convb[:, blk:blk + 1]), "acv4", 1)

                @block.sync
                def _(s, ci=ci, r0=r0, wdt=wdt):
                    W(s, "acv4")
                    lo = max(0, 4 - r0)
                    npos = wdt - lo
                    if npos > 0:
                        p0 = r0 - 4 + lo
                        inc(s.dma_start(
                            out=xcq[:, :].rearrange("p (b t) -> p b t", b=16)
                            [:, :, p0:p0 + npos],
                            in_=xconv[:, :].rearrange("p (b t) -> p b t", b=16)
                            [:, :, lo:lo + npos]), "xcst4", 16)

            if debug:
                @block.sync
                def _(s):
                    W(s, "xcst4")
                    W(s, "zst4")
                    inc(s.dma_start(out=dbg["xcq"], in_=xcq[:, :]), "dbgX", 16)
                    inc(s.dma_start(out=dbg["zq"], in_=zq[:, :]), "dbgX", 16)

        if stages < 3:
            @block.sync
            def _(s):
                W(s, "xcst4")
                W(s, "zst4")
                if debug:
                    for tg in ("dbgX",):
                        if tg in k.n:
                            W(s, tg)
            return nc
        with nc.sbuf_tensor("dt2", [128, 8 * TC2], F32) as dt2, \
             nc.sbuf_tensor("u2", [128, 8 * TC2], F32) as u2, \
             nc.sbuf_tensor("bcr2", [128, TC2], F32) as bcr2, \
             nc.sbuf_tensor("bb", [128, TC2], F32) as bb, \
             nc.sbuf_tensor("cc_", [128, TC2], F32) as cc_, \
             nc.sbuf_tensor("rr", [128, 2 * TC2], F32) as rr, \
             nc.sbuf_tensor("ww", [128, TC2], F32) as ww, \
             nc.sbuf_tensor("hh", [128, TC2], F32) as hh, \
             nc.sbuf_tensor("mm_", [128, TC2], F32) as mm_, \
             nc.sbuf_tensor("yall", [128, 8 * TC2], F32) as yall, \
             nc.sbuf_tensor("ytok", [128, 2 * 1024], F32) as ytok:

            @block.vector
            def _(v):
                inc(v.memset(ytok[0:1, :], 0.0), "zrv", 1)

            @block.sync
            def _(s):
                W(s, "zrv")
                inc(s.dma_start(out=ystage[0:1, :], in_=ytok[0:1, 0:DI]), "zr", 16)

            nscan = 0
            for c2 in range(NCH2):
                for wv in range(2):
                    cw = c2 * 2 + wv

                    @block.sync
                    def _(s, c2=c2, wv=wv, cw=cw):
                        if cw == 0:
                            s.wait_ge(sem("sp1"), 48 * NCH1)
                            s.wait_ge(sem("xcst4"), N["xcst4"])
                            s.wait_ge(sem("zst4"), N["zst4"])
                            s.wait_ge(sem("acv4"), N["acv4"])
                            s.wait_ge(sem("vmarg4"), N["vmarg4"])
                        else:
                            s.wait_ge(sem("ract"), 128 * cw)
                            s.wait_ge(sem("ydone"), cw)
                            if debug and "dbgS" in k.n:
                                s.wait_ge(sem("dbgS"), N["dbgS"])
                        src = dtbuf[:, :].rearrange("p (b t) -> p b t", b=16)
                        inc(s.dma_start(
                            out=dt2[:, :].rearrange("p (b t) -> p b t", b=8),
                            in_=src[:, wv * 8:wv * 8 + 8, c2 * TC2:(c2 + 1) * TC2]),
                            "s2ld", 16)
                        src = ubuf[:, :].rearrange("p (b t) -> p b t", b=16)
                        inc(s.dma_start(
                            out=u2[:, :].rearrange("p (b t) -> p b t", b=8),
                            in_=src[:, wv * 8:wv * 8 + 8, c2 * TC2:(c2 + 1) * TC2]),
                            "s2ld", 16)
                        if wv == 0:
                            if c2 > 0:
                                s.wait_ge(sem("ydone"), cw)
                            inc(s.dma_start(out=bcr2[:, :],
                                            in_=bcbuf[:, c2 * TC2:(c2 + 1) * TC2]),
                                "s2ldb", 16)
                        if debug and cw <= 1:
                            s.wait_ge(sem("s2ld"), 32 * (cw + 1))
                            if cw == 0:
                                s.wait_ge(sem("s2ldb"), 16)
                                inc(s.dma_start(out=dbg["bcr2c0"],
                                                in_=bcr2[:, :]), "dbgS", 16)
                            inc(s.dma_start(out=dbg[f"dt2w{cw}"],
                                            in_=dt2[:, :]), "dbgS", 16)
                            inc(s.dma_start(out=dbg[f"u2w{cw}"],
                                            in_=u2[:, :]), "dbgS", 16)

                    @block.vector
                    def _(v, c2=c2, wv=wv, cw=cw, nscan=nscan):
                        v.wait_ge(sem("s2ld"), 32 * (cw + 1))
                        v.wait_ge(sem("s2ldb"), 16 * (c2 + 1))
                        if cw > 0:
                            v.wait_ge(sem("yttr"), 64 * cw)
                        for n in range(NLOC):
                            v.stream_shuffle(out=bb[:, :], in_=bcr2[:, :],
                                             mask=[n] * 32)
                            v.stream_shuffle(out=cc_[:, :], in_=bcr2[:, :],
                                             mask=[16 + n] * 32)
                            for bl in range(8):
                                it = nscan + n * 8 + bl
                                col = (wv * 8 + bl) * NLOC + n
                                v.wait_ge(sem("ract"), it + 1)
                                v.tensor_mul(out=ww[:, :],
                                             in0=u2[:, bl * TC2:(bl + 1) * TC2],
                                             in1=bb[:, :])
                                v.tensor_tensor_scan(
                                    out=hh[:, :],
                                    data0=rr[:, (it % 2) * TC2:(it % 2 + 1) * TC2],
                                    data1=ww[:, :],
                                    initial=states[:, col:col + 1],
                                    op0=ALU.mult, op1=ALU.add)
                                inc(v.engine_nop(), "vscan", 1)
                                v.tensor_mul(out=mm_[:, :], in0=hh[:, :],
                                             in1=cc_[:, :])
                                v.tensor_copy(out=states[:, col:col + 1],
                                              in_=hh[:, TC2 - 1:TC2])
                                if n == 0:
                                    v.tensor_copy(
                                        out=yall[:, bl * TC2:(bl + 1) * TC2],
                                        in_=mm_[:, :])
                                else:
                                    v.tensor_add(
                                        out=yall[:, bl * TC2:(bl + 1) * TC2],
                                        in0=yall[:, bl * TC2:(bl + 1) * TC2],
                                        in1=mm_[:, :])
                        inc(v.engine_nop(), "ydone", 1)

                    @block.scalar
                    def _(a, c2=c2, wv=wv, cw=cw, nscan=nscan):
                        a.wait_ge(sem("s2ld"), 32 * (cw + 1))
                        if cw == 0:
                            W(a, "anegd")
                        for n in range(NLOC):
                            for bl in range(8):
                                it = nscan + n * 8 + bl
                                if it >= 2:
                                    a.wait_ge(sem("vscan"), it - 1)
                                inc(a.activation(
                                    out=rr[:, (it % 2) * TC2:(it % 2 + 1) * TC2],
                                    in_=dt2[:, bl * TC2:(bl + 1) * TC2],
                                    func=AF.Exp,
                                    scale=aneg[:, (wv * 8 + bl) * NLOC + n:
                                               (wv * 8 + bl) * NLOC + n + 1]),
                                    "ract", 1)

                    nscan += NLOC * 8

                    @block.tensor
                    def _(t, cw=cw):
                        t.wait_ge(sem("ydone"), cw + 1)
                        for tt in range(8):
                            for bl in range(8):
                                idx = tt * 8 + bl
                                g = 64 * cw + idx
                                if g >= 2:
                                    t.wait_ge(sem("ytev"), g - 1)
                                bank = ps[2 + (g % 2)]
                                inc(t.transpose(
                                    bank[:, 0:128],
                                    yall[:, bl * TC2 + tt * 128:
                                         bl * TC2 + (tt + 1) * 128],
                                    eye[:, :]), "yttr", 1)

                    @block.scalar
                    def _(a, cw=cw):
                        if cw == 0:
                            W(a, "zr")
                        for tt in range(8):
                            half = tt % 2
                            if cw * 8 + tt >= 2:
                                a.wait_ge(sem("yst"), 16 * (cw * 8 + tt - 1))
                                a.wait_ge(sem("ystx"), N["ystx"])
                            for bl in range(8):
                                idx = tt * 8 + bl
                                g = 64 * cw + idx
                                a.wait_ge(sem("yttr"), g + 1)
                                bank = ps[2 + (g % 2)]
                                inc(a.activation(
                                    out=ytok[:, half * 1024 + bl * 128:
                                             half * 1024 + (bl + 1) * 128],
                                    in_=bank[:, 0:128],
                                    func=AF.Copy), "ytev", 1)

                    @block.sync
                    def _(s, c2=c2, wv=wv, cw=cw):
                        for tt in range(8):
                            half = tt % 2
                            s.wait_ge(sem("ytev"), 64 * cw + (tt + 1) * 8)
                            tg = c2 * TC2 + tt * 128
                            qq = tg // QT
                            row = qq * HQ + 1 + (tg % QT)
                            inc(s.dma_start(
                                out=ystage[row:row + 128,
                                           wv * 1024:(wv + 1) * 1024],
                                in_=ytok[:, half * 1024:(half + 1) * 1024]),
                                "yst", 16)
                            if (tg + 128) % QT == 0 and (tg + 128) < T:
                                q2 = (tg + 128) // QT
                                inc(s.dma_start(
                                    out=ystage[q2 * HQ:q2 * HQ + 1,
                                               wv * 1024:(wv + 1) * 1024],
                                    in_=ytok[127:128, half * 1024:(half + 1) * 1024]),
                                    "ystx", 16)

            NYST = N["yst"]
            NYSTX = N["ystx"]

        if stages < 4:
            @block.sync
            def _(s):
                s.wait_ge(sem("yst"), NYST)
                s.wait_ge(sem("ystx"), NYSTX)
                if debug:
                    for tg in ("dbgX",):
                        if tg in k.n:
                            W(s, tg)
            return nc
        @block.gpsimd
        def _(g):
            g.wait_ge(sem("yst"), NYST)
            g.wait_ge(sem("ystx"), NYSTX)
            W(g, "zr")
            inc(g.collective_compute(
                "ReduceScatter", ALU.add, ins=[ystage[:, :]], outs=[yq[:, :]],
                replica_groups=[[0, 1, 2, 3], [4, 5, 6, 7]]), "rs", 1)

        if debug:
            @block.sync
            def _(s):
                s.wait_ge(sem("yst"), NYST)
                s.wait_ge(sem("ystx"), NYSTX)
                W(s, "zr")
                inc(s.dma_start(out=dbg["ystage"], in_=ystage[:, :]), "dbgX", 16)
                W(s, "rs")
                inc(s.dma_start(out=dbg["yq"], in_=yq[:, :]), "dbgX", 16)

        if stages < 5:
            @block.sync
            def _(s):
                W(s, "rs")
                if debug:
                    for tg in ("dbgX",):
                        if tg in k.n:
                            W(s, tg)
            return nc
        with nc.sbuf_tensor("ytk4", [128, 2 * DI], F32) as ytk4, \
             nc.sbuf_tensor("yfm", [128, 16 * 513], F32) as yfm, \
             nc.sbuf_tensor("xzt", [128, 2 * 2 * 513], F32) as xzt, \
             nc.sbuf_tensor("xtk", [128, 2 * DM], F32) as xtk, \
             nc.sbuf_tensor("xfm", [128, 8 * 513], F32) as xfm, \
             nc.sbuf_tensor("xmid", [128, 8 * 513], F32) as xmid, \
             nc.sbuf_tensor("h2", [128, 8 * 513], F32) as h2, \
             nc.sbuf_tensor("sr4", [1, 1600], F32) as sr4, \
             nc.sbuf_tensor("dif", [128, 512], F32) as dif, \
             nc.sbuf_tensor("xr", [128, 8 * 513], F32) as xr, \
             nc.sbuf_tensor("xk", [128, 8 * 512], F32) as xk, \
             nc.sbuf_tensor("k2", [128, 8 * 512], F32) as k2, \
             nc.sbuf_tensor("rv", [128, 8 * 512], F32) as rv, \
             nc.sbuf_tensor("wb4", [128, 2 * 16 * 128], F32) as wb4:

            for k4 in range(min(4, S4TILES)):
                P0 = 512 * k4

                @block.sync
                def _(s, k4=k4, P0=P0):
                    if k4 == 0:
                        W(s, "rs")
                    else:
                        if S4PART >= 2:
                            s.wait_ge(sem("gat"), 16 * k4)
                        elif S4SUB >= 2:
                            s.wait_ge(sem("yev4"), 64 * k4)
                        if S4PART >= 3:
                            s.wait_ge(sem("woev"), 8 * k4)
                        elif S4SUB >= 3:
                            s.wait_ge(sem("xev4"), 32 * k4)
                    for tt in range(4):
                        half = tt % 2
                        if tt >= 2:
                            s.wait_ge(sem("ytr4"), 64 * k4 + (tt - 1) * 16)
                        inc(s.dma_start(
                            out=ytk4[:, half * DI:(half + 1) * DI],
                            in_=yq[P0 + tt * 128:P0 + (tt + 1) * 128, :]),
                            "yld4", 16)
                    if S4HALO:
                        inc(s.dma_start(
                            out=yfm[:, 512:16 * 513:513],
                            in_=yq[P0 + 512:P0 + 513, :]
                            .rearrange("r (b p) -> (r p) b", p=128),
                            allow_slow_non_contiguous=True), "yxc4", 16)
                    for tt in range(4):
                        half = tt % 2
                        if tt >= 2:
                            s.wait_ge(sem("xtr4"), 32 * k4 + (tt - 1) * 8)
                        inc(s.dma_start(
                            out=xtk[:, half * DM:(half + 1) * DM],
                            in_=x_q[4 + P0 + tt * 128:4 + P0 + (tt + 1) * 128, :]),
                            "xld4b", 16)
                    if S4HALO:
                        inc(s.dma_start(
                            out=xfm[:, 512:8 * 513:513],
                            in_=x_q[4 + P0 + 512:4 + P0 + 513, :]
                            .rearrange("r (b p) -> (r p) b", p=128),
                            allow_slow_non_contiguous=True), "xxc4", 16)

                @block.tensor
                def _(t, k4=k4):
                    if k4 > 0:
                        if S4PART >= 6:
                            t.wait_ge(sem("oev"), 32 * k4)
                        if S4PART >= 4:
                            t.wait_ge(sem("bcev4"), k4)
                    if S4SUB < 2:
                        return
                    for tt in range(4):
                        half = tt % 2
                        t.wait_ge(sem("yld4"), 64 * k4 + (tt + 1) * 16)
                        for bl in range(16):
                            idx = tt * 16 + bl
                            if idx >= 2:
                                t.wait_ge(sem("yev4"), 64 * k4 + idx - 1)
                            bank = ps[2 + (idx % 2)]
                            inc(t.transpose(
                                bank[:, 0:128],
                                ytk4[:, half * DI + bl * 128:
                                     half * DI + (bl + 1) * 128],
                                eye[:, :]), "ytr4", 1)
                    if S4SUB < 3:
                        return
                    for tt in range(4):
                        half = tt % 2
                        t.wait_ge(sem("xld4b"), 64 * k4 + (tt + 1) * 16)
                        for jj in range(8):
                            idx = tt * 8 + jj
                            if idx >= 2:
                                t.wait_ge(sem("xev4"), 32 * k4 + idx - 1)
                            bank = ps[6] if idx % 2 == 0 else ps[7]
                            inc(t.transpose(
                                bank[:, 0:128],
                                xtk[:, half * DM + jj * 128:
                                    half * DM + (jj + 1) * 128],
                                eye[:, :]), "xtr4", 1)

                @block.scalar
                def _(a, k4=k4):
                    if S4SUB < 2:
                        return
                    for tt in range(4):
                        for bl in range(16):
                            idx = tt * 16 + bl
                            a.wait_ge(sem("ytr4"), 64 * k4 + idx + 1)
                            bank = ps[2 + (idx % 2)]
                            inc(a.activation(
                                out=yfm[:, bl * 513 + tt * 128:
                                        bl * 513 + (tt + 1) * 128],
                                in_=bank[:, 0:128],
                                func=AF.Copy), "yev4", 1)
                    if S4SUB < 3:
                        return
                    for tt in range(4):
                        for jj in range(8):
                            idx = tt * 8 + jj
                            a.wait_ge(sem("xtr4"), 32 * k4 + idx + 1)
                            bank = ps[6] if idx % 2 == 0 else ps[7]
                            inc(a.activation(
                                out=xfm[:, jj * 513 + tt * 128:
                                        jj * 513 + (tt + 1) * 128],
                                in_=bank[:, 0:128],
                                func=AF.Copy), "xev4", 1)

                if S4PART < 2:
                    continue
                @block.sync
                def _(s, k4=k4, P0=P0):
                    for bl in range(16):
                        half = bl % 2
                        if bl >= 2:
                            s.wait_ge(sem("gat"), 16 * k4 + bl - 1)
                        inc(s.dma_start(
                            out=xzt[:, half * 1026:half * 1026 + 513],
                            in_=xcq[:, bl * NPOS + P0:bl * NPOS + P0 + 513]),
                            "xzl4", 16)
                        inc(s.dma_start(
                            out=xzt[:, half * 1026 + 513:half * 1026 + 1026],
                            in_=zq[:, bl * NPOS + P0:bl * NPOS + P0 + 513]),
                            "xzl4", 16)

                @block.vector
                def _(v, k4=k4):
                    v.wait_ge(sem("yev4"), N["yev4"])
                    v.wait_ge(sem("yxc4"), N["yxc4"])
                    base = N["xzl4"] - 32 * 16
                    for bl in range(16):
                        half = bl % 2
                        v.wait_ge(sem("xzl4"), base + 32 * (bl + 1))
                        v.scalar_tensor_tensor(
                            out=yfm[:, bl * 513:(bl + 1) * 513],
                            in0=xzt[:, half * 1026:half * 1026 + 513],
                            scalar=dvec[:, bl:bl + 1],
                            in1=yfm[:, bl * 513:(bl + 1) * 513],
                            op0=ALU.mult, op1=ALU.add)
                        v.tensor_mul(
                            out=yfm[:, bl * 513:(bl + 1) * 513],
                            in0=yfm[:, bl * 513:(bl + 1) * 513],
                            in1=xzt[:, half * 1026 + 513:half * 1026 + 1026])
                        inc(v.engine_nop(), "gat", 1)

                if S4PART < 3:
                    continue
                @block.sync
                def _(s, k4=k4):
                    for j in range(8):
                        g4 = N["womm"] + j
                        if j < 2 and k4 > 0:
                            s.wait_ge(sem("fmm"), N["fmm"])
                        if j >= 2 or k4 > 0:
                            s.wait_ge(sem("womm"), g4 - 1)
                        half = j % 2
                        for kk in range(16):
                            inc(s.dma_start(
                                out=wb4[:, half * DI + kk * 128:
                                        half * DI + (kk + 1) * 128],
                                in_=w_out[kk * 128:(kk + 1) * 128,
                                          j * 128:(j + 1) * 128]), "ldwo", 16)

                @block.tensor
                def _(t, k4=k4):
                    W(t, "gat")
                    t.wait_ge(sem("xev4"), 32 * (k4 + 1))
                    base = N["womm"]
                    for j in range(8):
                        half = j % 2
                        t.wait_ge(sem("ldwo"), N["ldwo"] - (7 - j) * 16 * 16)
                        if base + j >= 2:
                            t.wait_ge(sem("woev"), base + j - 1)
                        bank = ps[4] if half == 0 else ps[5]
                        for kk in range(16):
                            t.matmul(bank[:, 0:512],
                                     wb4[:, half * DI + kk * 128:
                                         half * DI + (kk + 1) * 128],
                                     yfm[:, kk * 513:kk * 513 + 512],
                                     start=(kk == 0), stop=(kk == 15))
                        bank1 = ps[6] if half == 0 else ps[7]
                        for kk in range(16):
                            mi = t.matmul(bank1[:, 0:1],
                                          wb4[:, half * DI + kk * 128:
                                              half * DI + (kk + 1) * 128],
                                          yfm[:, kk * 513 + 512:kk * 513 + 513],
                                          start=(kk == 0), stop=(kk == 15))
                        inc(mi, "womm", 1)

                @block.vector
                def _(v, k4=k4):
                    v.wait_ge(sem("xev4"), N["xev4"])
                    v.wait_ge(sem("xxc4"), N["xxc4"])
                    base = N["womm"] - 8
                    for j in range(8):
                        half = j % 2
                        bank = ps[4] if half == 0 else ps[5]
                        v.wait_ge(sem("womm"), base + j + 1)
                        v.scalar_tensor_tensor(
                            out=xmid[:, j * 513:j * 513 + 512],
                            in0=bank[:, 0:512], scalar=mod2[:, GA * 8 + j:GA * 8 + j + 1],
                            in1=xfm[:, j * 513:j * 513 + 512],
                            op0=ALU.mult, op1=ALU.add)
                        v.scalar_tensor_tensor(
                            out=xmid[:, j * 513 + 512:j * 513 + 513],
                            in0=(ps[6] if half == 0 else ps[7])[:, 0:1],
                            scalar=mod2[:, GA * 8 + j:GA * 8 + j + 1],
                            in1=xfm[:, j * 513 + 512:j * 513 + 513],
                            op0=ALU.mult, op1=ALU.add)
                        inc(v.engine_nop(), "woev", 1)

                if S4PART < 4:
                    continue
                @block.scalar
                def _(a, k4=k4):
                    W(a, "woev")
                    if k4 > 0:
                        a.wait_ge(sem("fmm"), 24 * k4)
                    inc(a.activation(out=xr[:, 0:8 * 513], in_=xmid[:, :],
                                     func=AF.Square), "sq4d", 1)

                @block.tensor
                def _(t, k4=k4):
                    W(t, "sq4d")
                    W(t, "woev")
                    if k4 > 0:
                        t.wait_ge(sem("bcev4"), N["bcev4"])
                    for (srcT, b512, b1) in ((xmid, ps[7], ps[4]),
                                             (xr, ps[6], ps[5])):
                        for kk in range(8):
                            t.matmul(b512[0:1, 0:512], onesc[:, 0:1],
                                     srcT[:, kk * 513:kk * 513 + 512],
                                     start=(kk == 0), stop=(kk == 7))
                        for kk in range(8):
                            mi = t.matmul(b1[0:1, 0:1], onesc[:, 0:1],
                                          srcT[:, kk * 513 + 512:kk * 513 + 513],
                                          start=(kk == 0), stop=(kk == 7))
                    inc(mi, "lnmm", 1)

                @block.scalar
                def _(a, k4=k4):
                    W(a, "lnmm")
                    a.activation(out=sr4[0:1, 0:512], in_=ps[7][0:1, 0:512],
                                 func=AF.Copy)
                    a.activation(out=sr4[0:1, 512:513], in_=ps[4][0:1, 0:1],
                                 func=AF.Copy)
                    a.activation(out=sr4[0:1, 513:1025], in_=ps[6][0:1, 0:512],
                                 func=AF.Copy)
                    inc(a.activation(out=sr4[0:1, 1025:1026], in_=ps[5][0:1, 0:1],
                                     func=AF.Copy), "lnev", 1)

                @block.vector
                def _(v, k4=k4):
                    W(v, "lnev")
                    v.tensor_scalar_mul(out=sr4[0:1, 1026:1539],
                                        in0=sr4[0:1, 0:513], scalar1=-1.0 / DM)
                    v.tensor_scalar_mul(out=sr4[0:1, 0:513],
                                        in0=sr4[0:1, 513:1026], scalar1=1.0 / DM)
                    v.tensor_mul(out=sr4[0:1, 513:1026], in0=sr4[0:1, 1026:1539],
                                 in1=sr4[0:1, 1026:1539])
                    inc(v.tensor_sub(out=sr4[0:1, 0:513], in0=sr4[0:1, 0:513],
                                     in1=sr4[0:1, 513:1026]), "lnvar", 1)

                @block.scalar
                def _(a, k4=k4):
                    W(a, "lnvar")
                    inc(a.activation(out=sr4[0:1, 513:1026], in_=sr4[0:1, 0:513],
                                     func=AF.Sqrt, bias=epsv[0:1, 0:1]), "lnrs", 1)

                @block.vector
                def _(v, k4=k4):
                    W(v, "lnrs")
                    inc(v.reciprocal(out=sr4[0:1, 513:1026],
                                     in_=sr4[0:1, 513:1026]), "lnrc", 1)

                @block.tensor
                def _(t, k4=k4):
                    W(t, "lnrc")
                    t.matmul(ps[2][:, 0:512], onesr[0:1, :], sr4[0:1, 1026:1538],
                             start=True, stop=True)
                    t.matmul(ps[4][:, 0:1], onesr[0:1, :], sr4[0:1, 1538:1539],
                             start=True, stop=True)
                    t.matmul(ps[3][:, 0:512], onesr[0:1, :], sr4[0:1, 513:1025],
                             start=True, stop=True)
                    inc(t.matmul(ps[5][:, 0:1], onesr[0:1, :], sr4[0:1, 1025:1026],
                                 start=True, stop=True), "bcmm", 1)

                @block.vector
                def _(v, k4=k4):
                    W(v, "bcmm")
                    if k4 > 0:
                        v.wait_ge(sem("ost"), 64 * k4)
                    for j in range(8):
                        v.tensor_add(out=h2[:, j * 513:j * 513 + 512],
                                     in0=xmid[:, j * 513:j * 513 + 512],
                                     in1=ps[2][:, 0:512])
                        v.tensor_add(out=h2[:, j * 513 + 512:j * 513 + 513],
                                     in0=xmid[:, j * 513 + 512:j * 513 + 513],
                                     in1=ps[4][:, 0:1])
                        v.tensor_mul(out=h2[:, j * 513:j * 513 + 512],
                                     in0=h2[:, j * 513:j * 513 + 512],
                                     in1=ps[3][:, 0:512])
                        v.tensor_mul(out=h2[:, j * 513 + 512:j * 513 + 513],
                                     in0=h2[:, j * 513 + 512:j * 513 + 513],
                                     in1=ps[5][:, 0:1])
                    inc(v.engine_nop(), "bcev4", 1)

                @block.scalar
                def _(a, k4=k4):
                    W(a, "bcev4")
                    for j in range(8):
                        inc(a.activation(out=h2[:, j * 513:(j + 1) * 513],
                                         in_=h2[:, j * 513:(j + 1) * 513],
                                         func=AF.Identity,
                                         scale=mod2[:, CB1 * 8 + j:CB1 * 8 + j + 1],
                                         bias=mod2[:, SB * 8 + j:SB * 8 + j + 1]),
                                         "h2d", 1)

                if k4 == 0:
                    @block.vector
                    def _(v):
                        W(v, "h2d")
                        v.tensor_scalar_mul(
                            out=h2[:, 0:8 * 513:513],
                            in0=h2[:, 0:8 * 513:513],
                            scalar1=halosc[:, 0:1])
                        inc(v.tensor_scalar_mul(out=dif[:, 0:1], in0=dif[:, 0:1],
                                                scalar1=1.0), "h2h", 1)
                else:
                    @block.vector
                    def _(v):
                        inc(v.engine_nop(), "h2h", 1)

                if debug and k4 == 0:
                    @block.sync
                    def _(s):
                        W(s, "h2h")
                        inc(s.dma_start(out=dbg["xmid"], in_=xmid[:, :]),
                            "dbgX", 16)
                        inc(s.dma_start(out=dbg["h2"], in_=h2[:, :]), "dbgX", 16)

                if S4PART < 5:
                    continue
                @block.vector
                def _(v, k4=k4):
                    W(v, "h2d")
                    W(v, "h2h")
                    for j in range(8):
                        v.tensor_sub(out=dif[:, :],
                                     in0=h2[:, j * 513 + 1:j * 513 + 513],
                                     in1=h2[:, j * 513:j * 513 + 512])
                        v.scalar_tensor_tensor(
                            out=xr[:, j * 513:j * 513 + 512],
                            in0=dif[:, :], scalar=mur[:, j:j + 1],
                            in1=h2[:, j * 513:j * 513 + 512],
                            op0=ALU.mult, op1=ALU.add)
                        v.scalar_tensor_tensor(
                            out=xk[:, j * 512:(j + 1) * 512],
                            in0=dif[:, :], scalar=muk[:, j:j + 1],
                            in1=h2[:, j * 513:j * 513 + 512],
                            op0=ALU.mult, op1=ALU.add)
                    inc(v.engine_nop(), "xrk", 1)

                for wi, (wt, rhs_, dsts) in enumerate((
                        (wrT, "xr", "rv"), (wkT, "xk", "k2"), (wvT, "k2", None))):
                    @block.sync
                    def _(s, k4=k4, wi=wi, wt=wt):
                        for j in range(8):
                            g4 = N["fmm"] + j
                            if wi == 0 and j < 2:
                                s.wait_ge(sem("womm"), N["womm"])
                            if g4 >= 2:
                                s.wait_ge(sem("fmm"), g4 - 1)
                            half = j % 2
                            for kk in range(8):
                                inc(s.dma_start(
                                    out=wb4[:, half * DI + kk * 128:
                                            half * DI + (kk + 1) * 128],
                                    in_=wt[kk * 128:(kk + 1) * 128,
                                           j * 128:(j + 1) * 128]), "ldwf", 16)

                    @block.tensor
                    def _(t, k4=k4, wi=wi, rhs_=rhs_):
                        rhs = {"xr": xr, "xk": xk, "k2": k2}[rhs_]
                        stride = 513 if rhs_ == "xr" else 512
                        if wi == 0:
                            W(t, "xrk")
                            W(t, "bcev4")
                        else:
                            W(t, "fev")
                        base = N["fmm"]
                        for j in range(8):
                            half = j % 2
                            t.wait_ge(sem("ldwf"), N["ldwf"] - (7 - j) * 8 * 16)
                            if base + j >= 2:
                                t.wait_ge(sem("fev"), base + j - 1)
                            bank = ps[4] if half == 0 else ps[5]
                            for kk in range(8):
                                mi = t.matmul(bank[:, 0:512],
                                              wb4[:, half * DI + kk * 128:
                                                  half * DI + (kk + 1) * 128],
                                              rhs[:, kk * stride:kk * stride + 512],
                                              start=(kk == 0), stop=(kk == 7))
                            inc(mi, "fmm", 1)

                    if wi == 0:
                        @block.scalar
                        def _(a, k4=k4):
                            base = N["fmm"] - 8
                            for j in range(8):
                                half = j % 2
                                bank = ps[4] if half == 0 else ps[5]
                                a.wait_ge(sem("fmm"), base + j + 1)
                                inc(a.activation(out=rv[:, j * 512:(j + 1) * 512],
                                                 in_=bank[:, 0:512],
                                                 func=AF.Sigmoid), "fev", 1)
                    elif wi == 1:
                        @block.scalar
                        def _(a, k4=k4):
                            base = N["fmm"] - 8
                            for j in range(8):
                                half = j % 2
                                bank = ps[4] if half == 0 else ps[5]
                                a.wait_ge(sem("fmm"), base + j + 1)
                                mi = a.activation(out=k2[:, j * 512:(j + 1) * 512],
                                                  in_=bank[:, 0:512], func=AF.Relu)
                                inc(mi, "krl", 1)
                                W(a, "krl")
                                inc(a.activation(out=k2[:, j * 512:(j + 1) * 512],
                                                 in_=k2[:, j * 512:(j + 1) * 512],
                                                 func=AF.Square), "fev", 1)
                    else:
                        @block.vector
                        def _(v, k4=k4):
                            base = N["fmm"] - 8
                            for j in range(8):
                                half = j % 2
                                bank = ps[4] if half == 0 else ps[5]
                                v.wait_ge(sem("fmm"), base + j + 1)
                                v.tensor_mul(out=rv[:, j * 512:(j + 1) * 512],
                                             in0=rv[:, j * 512:(j + 1) * 512],
                                             in1=bank[:, 0:512])
                                v.scalar_tensor_tensor(
                                    out=rv[:, j * 512:(j + 1) * 512],
                                    in0=rv[:, j * 512:(j + 1) * 512],
                                    scalar=mod2[:, GB * 8 + j:GB * 8 + j + 1],
                                    in1=xmid[:, j * 513 + 1:j * 513 + 513],
                                    op0=ALU.mult, op1=ALU.add)
                                inc(v.engine_nop(), "fev", 1)

                if S4PART < 6:
                    continue
                @block.tensor
                def _(t, k4=k4):
                    W(t, "fev")
                    for tt in range(4):
                        for jj in range(8):
                            idx = tt * 8 + jj
                            if N["otr"] >= 2:
                                t.wait_ge(sem("oev"), N["otr"] - 1)
                            bank = ps[2 + (N["otr"] % 2)]
                            inc(t.transpose(
                                bank[:, 0:128],
                                rv[:, jj * 512 + tt * 128:jj * 512 + (tt + 1) * 128],
                                eye[:, :]), "otr", 1)

                @block.scalar
                def _(a, k4=k4):
                    base = N["otr"] - 32
                    for tt in range(4):
                        half = tt % 2
                        if 4 * k4 + tt >= 2:
                            a.wait_ge(sem("ost"), 16 * (4 * k4 + tt - 1))
                        for jj in range(8):
                            idx = tt * 8 + jj
                            a.wait_ge(sem("otr"), base + idx + 1)
                            bank = ps[2 + ((base + idx) % 2)]
                            inc(a.activation(
                                out=h2[:, half * DM + jj * 128:
                                       half * DM + (jj + 1) * 128],
                                in_=bank[:, 0:128],
                                func=AF.Copy), "oev", 1)

                @block.sync
                def _(s, k4=k4, P0=P0):
                    for tt in range(4):
                        half = tt % 2
                        s.wait_ge(sem("oev"), N["oev"] - 8 * (3 - tt))
                        inc(s.dma_start(
                            out=out_q[P0 + tt * 128:P0 + (tt + 1) * 128, :],
                            in_=h2[:, half * DM:(half + 1) * DM]), "ost", 16)

            NOST = N["ost"] if S4PART >= 6 else 0

        @block.sync
        def _(s):
            if S4PART >= 6:
                s.wait_ge(sem("ost"), NOST)
            else:
                for tg in ("yev4", "xev4", "gat", "woev", "h2d", "fev"):
                    if tg in k.n and k.n[tg] > 0:
                        W(s, tg)
            if debug:
                for tg in ("dbgX", "dbgS"):
                    if tg in k.n:
                        s.wait_ge(sem(tg), N[tg])

    return nc



def _prep_inputs(inputs):
    x = np.ascontiguousarray(inputs["L0_attn_x"], np.float32)
    W_in = np.ascontiguousarray(inputs["L0_attn_W_in"], np.float32)
    conv_w = np.asarray(inputs["L0_attn_conv_w"], np.float32)
    conv_b = np.asarray(inputs["L0_attn_conv_b"], np.float32)
    W_x = np.ascontiguousarray(inputs["L0_attn_W_x"], np.float32)
    A_log = np.asarray(inputs["L0_attn_A_log"], np.float32)
    dt_bias = np.asarray(inputs["L0_attn_dt_bias"], np.float32)
    D = np.asarray(inputs["L0_attn_D"], np.float32)
    W_out = np.ascontiguousarray(inputs["L0_attn_W_out"], np.float32)
    W_r = np.ascontiguousarray(inputs["L0_ffn_W_r"], np.float32)
    W_k = np.ascontiguousarray(inputs["L0_ffn_W_k"], np.float32)
    W_v = np.ascontiguousarray(inputs["L0_ffn_W_v"], np.float32)
    mu_r = np.asarray(inputs["L0_ffn_mu_r"], np.float32)
    mu_k = np.asarray(inputs["L0_ffn_mu_k"], np.float32)
    c = np.asarray(inputs["L0_c"], np.float32)
    wmod_ = np.ascontiguousarray(inputs["L0_wmod"], np.float32)

    def pack16(vec):
        return np.ascontiguousarray(vec.reshape(16, 128).T)

    def pack8(vec):
        return np.ascontiguousarray(vec.reshape(8, 128).T)

    convw_p = np.ascontiguousarray(
        conv_w.reshape(16, 128, 4).transpose(1, 0, 2).reshape(128, 64))
    convb_p = pack16(conv_b)
    beta_p = pack16(dt_bias)
    dvec_p = pack16(D)
    mur_p = pack8(mu_r)
    muk_p = pack8(mu_k)
    eye_ = np.eye(128, dtype=np.float32)
    onesr_ = np.ones((1, 128), np.float32)
    onesc_ = np.ones((128, 1), np.float32)

    in_maps = []
    for r in range(8):
        b, q = r // 4, r % 4
        nsl = slice(16 * q, 16 * q + 16)
        wx_sl = np.ascontiguousarray(np.concatenate(
            [W_x[:, 0:64][:, nsl], W_x[:, 64:128][:, nsl], W_x[:, 128:129]],
            axis=1))
        alog_sl = A_log[:, nsl]
        alog_p = np.ascontiguousarray(
            alog_sl.reshape(16, 128, 16).transpose(1, 0, 2).reshape(128, 256))
        xq = np.zeros((XQR, DM), np.float32)
        lo = q * QT - 5
        src_lo = max(0, lo)
        xq[src_lo - lo:, :] = x[b, src_lo:(q + 1) * QT, :]
        halo = np.full((128, 1), 0.0 if q == 0 else 1.0, np.float32)
        in_maps.append({
            "x_b": x[b], "x_q": xq, "w_in": W_in, "wx_p": wx_sl,
            "w_out": W_out, "w_r": W_r, "w_k": W_k, "w_v": W_v,
            "wmod": wmod_, "convw_p": convw_p, "convb_p": convb_p,
            "beta_p": beta_p, "dvec_p": dvec_p, "alog_p": alog_p,
            "mur_p": mur_p, "muk_p": muk_p,
            "c_p": pack8(c[b]).copy(), "eye_in": eye_, "onesr_in": onesr_,
            "onesc_in": onesc_, "halo_in": halo,
        })
    return in_maps


_NC_CACHE = {}


def kernel(**inputs):
    in_maps = _prep_inputs(inputs)
    key = ("prog", DEBUG, STAGES, S1CH, S1PART, S4PART, S4HALO, S4TILES, S4SUB)
    if key not in _NC_CACHE:
        _NC_CACHE[key] = build_program(debug=DEBUG, stages=STAGES)
    nc = _NC_CACHE[key]
    res = run_bass_kernel_spmd(nc, in_maps, core_ids=list(range(8)))
    out = np.zeros((2, T, DM), np.float32)
    for r in range(8):
        b, q = r // 4, r % 4
        if "out_q" in res.results[r]:
            out[b, q * QT:(q + 1) * QT, :] = res.results[r]["out_q"]
    if DEBUG:
        kernel._dbg = res.results
    return out


# version 06 (final)
# speedup vs baseline: 1.0456x; 1.0456x over previous
import contextlib
import numpy as np

import concourse.bass as bass
import concourse.mybir as mybir
from concourse.bass_utils import run_bass_kernel_spmd

F32 = mybir.dt.float32
AF = mybir.ActivationFunctionType
ALU = mybir.AluOpType
AX = mybir.AxisListType

T = 8192
DM = 1024
DI = 2048
TC1 = 512
NCH1 = T // TC1
TC2 = 1024
NCH2 = T // TC2
QT = T // 4
MG = TC1 + 3
NLOC = 16
HQ = QT + 1
XQR = QT + 5
NPOS = QT + 1

DEBUG = False
STAGES = 5


class K:
    def __init__(self, nc, es):
        self.nc = nc
        self.es = es
        self.sems = {}
        self.n = {}

    def sem(self, name):
        if name not in self.sems:
            self.sems[name] = self.es.enter_context(self.nc.semaphore(name))
            self.n[name] = 0
        return self.sems[name]

    def inc(self, inst, name, amt=1):
        s = self.sem(name)
        self.n[name] += amt
        assert self.n[name] < 64000, f"sem {name} overflow"
        inst.then_inc(s, amt)

    def w(self, eng, name, val=None):
        s = self.sem(name)
        eng.wait_ge(s, self.n[name] if val is None else val)

    def __getitem__(self, name):
        self.sem(name)
        return self.n[name]


S1CH = 16
S1PART = 9
S4PART = 9
S4HALO = 1
S4TILES = 4
S4SUB = 9


def build_program(debug=False, stages=5):
    global S1CH, S1PART, S4PART, S4HALO, S4TILES, S4SUB
    nc = bass.Bass("TRN2", target_bir_lowering=False, debug=False,
                   detect_race_conditions=False)

    x_b = nc.dram_tensor("x_b", [T, DM], F32, kind="ExternalInput").ap()
    x_q = nc.dram_tensor("x_q", [XQR, DM], F32, kind="ExternalInput").ap()
    w_in = nc.dram_tensor("w_in", [DM, 2 * DI], F32, kind="ExternalInput").ap()
    wx_p = nc.dram_tensor("wx_p", [DI, 33], F32, kind="ExternalInput").ap()
    w_out = nc.dram_tensor("w_out", [DI, DM], F32, kind="ExternalInput").ap()
    w_r = nc.dram_tensor("w_r", [DM, DM], F32, kind="ExternalInput").ap()
    w_k = nc.dram_tensor("w_k", [DM, DM], F32, kind="ExternalInput").ap()
    w_v = nc.dram_tensor("w_v", [DM, DM], F32, kind="ExternalInput").ap()
    wmod = nc.dram_tensor("wmod", [DM, 6 * DM], F32, kind="ExternalInput").ap()
    convw_p = nc.dram_tensor("convw_p", [128, 64], F32, kind="ExternalInput").ap()
    convb_p = nc.dram_tensor("convb_p", [128, 16], F32, kind="ExternalInput").ap()
    beta_p = nc.dram_tensor("beta_p", [128, 16], F32, kind="ExternalInput").ap()
    dvec_p = nc.dram_tensor("dvec_p", [128, 16], F32, kind="ExternalInput").ap()
    alog_p = nc.dram_tensor("alog_p", [128, 16 * NLOC], F32, kind="ExternalInput").ap()
    mur_p = nc.dram_tensor("mur_p", [128, 8], F32, kind="ExternalInput").ap()
    muk_p = nc.dram_tensor("muk_p", [128, 8], F32, kind="ExternalInput").ap()
    c_p = nc.dram_tensor("c_p", [128, 8], F32, kind="ExternalInput").ap()
    eye_in = nc.dram_tensor("eye_in", [128, 128], F32, kind="ExternalInput").ap()
    onesr_in = nc.dram_tensor("onesr_in", [1, 128], F32, kind="ExternalInput").ap()
    onesc_in = nc.dram_tensor("onesc_in", [128, 1], F32, kind="ExternalInput").ap()
    halo_in = nc.dram_tensor("halo_in", [128, 1], F32, kind="ExternalInput").ap()

    out_q = nc.dram_tensor("out_q", [QT, DM], F32, kind="ExternalOutput").ap()

    dtbuf = nc.dram_tensor("dtbuf", [128, 16 * T], F32, kind="Internal").ap()
    ubuf = nc.dram_tensor("ubuf", [128, 16 * T], F32, kind="Internal").ap()
    bcbuf = nc.dram_tensor("bcbuf", [128, T], F32, kind="Internal").ap()
    xcq = nc.dram_tensor("xcq", [128, 16 * NPOS], F32, kind="Internal").ap()
    zq = nc.dram_tensor("zq", [128, 16 * NPOS], F32, kind="Internal").ap()
    ystage = nc.dram_tensor("ystage", [4 * HQ, DI], F32, kind="Internal").ap()
    yq = nc.dram_tensor("yq", [HQ, DI], F32, kind="Internal").ap()
    wrT = nc.dram_tensor("wrT", [DM, DM], F32, kind="Internal").ap()
    wkT = nc.dram_tensor("wkT", [DM, DM], F32, kind="Internal").ap()
    wvT = nc.dram_tensor("wvT", [DM, DM], F32, kind="Internal").ap()

    dbg = {}
    if debug:
        for nm, shp in (("mod2", [128, 48]), ("hch", [128, 8 * TC1]),
                        ("xconv", [128, 16 * TC1]), ("dt", [128, 16 * TC1]),
                        ("bc", [128, TC1]), ("ystage", [4 * HQ, DI]),
                        ("yq", [HQ, DI]), ("xcq", [128, 16 * NPOS]),
                        ("zq", [128, 16 * NPOS]), ("xmid", [128, 8 * 513]),
                        ("h2", [128, 8 * 513]),
                        ("dt2w0", [128, 8 * TC2]), ("u2w0", [128, 8 * TC2]),
                        ("dt2w1", [128, 8 * TC2]), ("u2w1", [128, 8 * TC2]),
                        ("bcr2c0", [128, TC2])):
            dbg[nm] = nc.dram_tensor("dbg_" + nm, shp, F32,
                                     kind="ExternalOutput").ap()

    es = contextlib.ExitStack()
    with es:
        e = es.enter_context
        k = K(nc, es)
        sem, inc, W = k.sem, k.inc, k.w
        N = k

        mod2 = e(nc.sbuf_tensor("mod2", [128, 48], F32))
        aneg = e(nc.sbuf_tensor("aneg", [128, 16 * NLOC], F32))
        convw = e(nc.sbuf_tensor("convw", [128, 64], F32))
        convb = e(nc.sbuf_tensor("convb", [128, 16], F32))
        beta = e(nc.sbuf_tensor("beta", [128, 16], F32))
        dvec = e(nc.sbuf_tensor("dvec", [128, 16], F32))
        mur = e(nc.sbuf_tensor("mur", [128, 8], F32))
        muk = e(nc.sbuf_tensor("muk", [128, 8], F32))
        eye = e(nc.sbuf_tensor("eye", [128, 128], F32))
        onesr = e(nc.sbuf_tensor("onesr", [1, 128], F32))
        onesc = e(nc.sbuf_tensor("onesc", [128, 1], F32))
        halosc = e(nc.sbuf_tensor("halosc", [128, 1], F32))
        states = e(nc.sbuf_tensor("states", [128, 16 * NLOC], F32))
        wxsb = e(nc.sbuf_tensor("wxsb", [128, 16 * 33], F32))
        epsv = e(nc.sbuf_tensor("epsv", [128, 1], F32))
        onep = e(nc.sbuf_tensor("onep", [128, 1], F32))

        ps = [e(nc.psum_tensor(f"ps{i}", [128, 512], F32)) for i in range(8)]

        block = e(nc.Block())
        SA, CA1, GA, SB, CB1, GB = 0, 1, 2, 3, 4, 5

        @block.sync
        def _(s):
            for dst, src in ((convw, convw_p), (convb, convb_p), (beta, beta_p),
                             (dvec, dvec_p), (mur, mur_p), (muk, muk_p),
                             (eye, eye_in), (onesr, onesr_in),
                             (onesc, onesc_in),
                             (halosc, halo_in), (aneg, alog_p)):
                inc(s.dma_start(out=dst[:, :], in_=src[:, :]), "ld0", 16)
            for kk in range(16):
                inc(s.dma_start(out=wxsb[:, kk * 33:(kk + 1) * 33],
                                in_=wx_p[kk * 128:(kk + 1) * 128, :]), "ld0", 16)

        @block.vector
        def _(v):
            W(v, "ld0")
            v.memset(states[:, :], 0.0)
            v.memset(onep[:, :], 1.0)
            inc(v.memset(epsv[:, :], 1e-5), "init", 1)

        @block.scalar
        def _(a):
            W(a, "ld0")
            inc(a.activation(out=aneg[:, :], in_=aneg[:, :], func=AF.Exp),
                "anege", 1)
            W(a, "anege")
            inc(a.activation(out=aneg[:, :], in_=aneg[:, :], func=AF.Copy,
                             scale=-1.0), "anegd", 1)

        if stages == 11:
            @block.sync
            def _(s):
                W(s, "ld0")
                W(s, "init")
            @block.scalar
            def _(a):
                W(a, "anegd")
            return nc

        with nc.sbuf_tensor("cs", [128, 8], F32) as cs, \
             nc.sbuf_tensor("wmt", [128, 256], F32) as wmt, \
             nc.sbuf_tensor("modraw", [128, 48], F32) as modraw, \
             nc.sbuf_tensor("gtmp", [128, 16], F32) as gtmp:

            @block.sync
            def _(s):
                inc(s.dma_start(out=cs[:, :], in_=c_p[:, :]), "ldc", 16)

            @block.scalar
            def _(a):
                W(a, "ldc")
                inc(a.activation(out=cs[:, :], in_=cs[:, :], func=AF.Silu),
                    "csd", 1)

            @block.sync
            def _(s):
                for oc in range(48):
                    for kk in range(8):
                        i2 = oc * 8 + kk
                        h = (i2 % 2) * 128
                        if i2 >= 2:
                            W(s, "modmm", i2 - 1)
                        inc(s.dma_start(out=wmt[:, h:h + 128],
                                        in_=wmod[kk * 128:(kk + 1) * 128,
                                                 oc * 128:(oc + 1) * 128]),
                            "ldwm", 16)

            @block.tensor
            def _(t):
                W(t, "csd")
                W(t, "ld0")
                for oc in range(48):
                    for kk in range(8):
                        i2 = oc * 8 + kk
                        h = (i2 % 2) * 128
                        t.wait_ge(sem("ldwm"), (i2 + 1) * 16)
                        if kk == 0 and oc > 0:
                            t.wait_ge(sem("modev"), oc)
                        inc(t.matmul(ps[0][:, 0:1], wmt[:, h:h + 128],
                                     cs[:, kk:kk + 1],
                                     start=(kk == 0), stop=(kk == 7)), "modmm", 1)

            @block.scalar
            def _(a):
                for oc in range(48):
                    a.wait_ge(sem("modmm"), (oc + 1) * 8)
                    inc(a.activation(out=modraw[:, oc:oc + 1], in_=ps[0][:, 0:1],
                                     func=AF.Copy), "modev", 1)

            @block.vector
            def _(v):
                W(v, "modev")
                v.tensor_copy(out=mod2[:, SA * 8:SA * 8 + 8], in_=modraw[:, 0:8])
                v.tensor_scalar_add(out=mod2[:, CA1 * 8:CA1 * 8 + 8],
                                    in0=modraw[:, 8:16], scalar1=1.0)
                v.tensor_copy(out=mod2[:, SB * 8:SB * 8 + 8], in_=modraw[:, 24:32])
                v.tensor_scalar_add(out=mod2[:, CB1 * 8:CB1 * 8 + 8],
                                    in0=modraw[:, 32:40], scalar1=1.0)
                v.tensor_single_scalar(out=gtmp[:, 0:8], in_=modraw[:, 16:24],
                                       scalar=0.0, op=ALU.is_ge)
                v.tensor_single_scalar(out=gtmp[:, 8:16], in_=modraw[:, 40:48],
                                       scalar=0.0, op=ALU.is_ge)
                v.tensor_scalar(out=gtmp[:, 0:8], in0=gtmp[:, 0:8], scalar1=2.0,
                                scalar2=-1.0, op0=ALU.mult, op1=ALU.add)
                inc(v.tensor_scalar(out=gtmp[:, 8:16], in0=gtmp[:, 8:16],
                                    scalar1=2.0, scalar2=-1.0, op0=ALU.mult,
                                    op1=ALU.add), "sgn", 1)

            @block.scalar
            def _(a):
                W(a, "sgn")
                a.activation(out=modraw[:, 16:24], in_=modraw[:, 16:24],
                             func=AF.Abs)
                inc(a.activation(out=modraw[:, 40:48], in_=modraw[:, 40:48],
                                 func=AF.Abs), "absd", 1)

            @block.vector
            def _(v):
                W(v, "absd")
                v.tensor_scalar_max(out=modraw[:, 16:24], in0=modraw[:, 16:24],
                                    scalar1=0.25)
                v.tensor_scalar_max(out=modraw[:, 40:48], in0=modraw[:, 40:48],
                                    scalar1=0.25)
                v.tensor_mul(out=mod2[:, GA * 8:GA * 8 + 8],
                             in0=modraw[:, 16:24], in1=gtmp[:, 0:8])
                inc(v.tensor_mul(out=mod2[:, GB * 8:GB * 8 + 8],
                                 in0=modraw[:, 40:48], in1=gtmp[:, 8:16]),
                    "modok", 1)

            if stages == 12:
                @block.sync
                def _(s):
                    W(s, "modok")
                return nc

            wtio = []
            for wi, (wsrc, wdst) in enumerate(((w_r, wrT), (w_k, wkT),
                                              (w_v, wvT))):
                for jj in range(8):
                    for cci in range(8):
                        wtio.append((wsrc[jj * 128:(jj + 1) * 128,
                                          cci * 128:(cci + 1) * 128],
                                     wdst[cci * 128:(cci + 1) * 128,
                                          jj * 128:(jj + 1) * 128]))

            @block.sync
            def _(s):
                W(s, "modev")
                for idx in range(193):
                    if idx < 192:
                        h = (idx % 2) * 128
                        if idx >= 2:
                            s.wait_ge(sem("wtst"), 16 * (idx - 1))
                        inc(s.dma_start(out=wmt[:, h:h + 128],
                                        in_=wtio[idx][0]), "wtld", 16)
                    if idx >= 1:
                        j = idx - 1
                        hj = (j % 2) * 128
                        s.wait_ge(sem("wtev"), j + 1)
                        inc(s.dma_start(out=wtio[j][1],
                                        in_=wmt[:, hj:hj + 128]), "wtst", 16)

            @block.tensor
            def _(t):
                for idx in range(192):
                    h = (idx % 2) * 128
                    t.wait_ge(sem("wtld"), 16 * (idx + 1))
                    if idx > 0:
                        t.wait_ge(sem("wtev"), idx)
                    inc(t.transpose(ps[1][:, 0:128], wmt[:, h:h + 128],
                                    eye[:, :]), "wttr", 1)

            @block.scalar
            def _(a):
                for idx in range(192):
                    h = (idx % 2) * 128
                    a.wait_ge(sem("wttr"), idx + 1)
                    inc(a.activation(out=wmt[:, h:h + 128], in_=ps[1][:, 0:128],
                                     func=AF.Copy), "wtev", 1)


        if stages == 13:
            @block.sync
            def _(s):
                W(s, "wtst")
                W(s, "modok")
            return nc

        def phase1(tag, xsrc, nrows, nchunks, widths, store):
            pass

        with nc.sbuf_tensor("xt", [128, 4 * DM], F32) as xt, \
             nc.sbuf_tensor("x2t", [128, DM], F32) as x2t, \
             nc.sbuf_tensor("stt", [128, 16], F32) as stt, \
             nc.sbuf_tensor("hch", [128, 8 * TC1], F32) as hch, \
             nc.sbuf_tensor("wbuf", [128, 2 * 8 * 128], F32) as wbuf, \
             nc.sbuf_tensor("xssm", [128, 16 * MG], F32) as xssm, \
             nc.sbuf_tensor("xconv", [128, 16 * TC1], F32) as xconv, \
             nc.sbuf_tensor("cacc", [128, 4 * TC1], F32) as cacc, \
             nc.sbuf_tensor("dt_c", [128, 16 * TC1], F32) as dt_c, \
             nc.sbuf_tensor("u_c", [128, 16 * TC1], F32) as u_c, \
             nc.sbuf_tensor("bcrep", [128, TC1], F32) as bcrep, \
             nc.sbuf_tensor("bc32", [32, TC1], F32) as bc32, \
             nc.sbuf_tensor("s_sb", [1, TC1], F32) as s_sb:

            @block.vector
            def _(v):
                for blk in range(16):
                    v.memset(xssm[:, blk * MG:blk * MG + 3], 0.0)

            for c in range(min(S1CH, NCH1)):
                @block.sync
                def _(s, c=c):
                    if c > 0:
                        s.wait_ge(sem("htr"), 32 * c)
                    for tt in range(4):
                        inc(s.dma_start(
                            out=xt[:, tt * DM:(tt + 1) * DM],
                            in_=x_b[c * TC1 + tt * 128:
                                    c * TC1 + (tt + 1) * 128, :]), "ldx", 16)

                for tt in range(4):
                    @block.scalar
                    def _(a, c=c, tt=tt):
                        a.wait_ge(sem("ldx"), (4 * c + tt + 1) * 16)
                        if 4 * c + tt > 0:
                            a.wait_ge(sem("vvar"), 4 * c + tt)
                        inc(a.activation(out=x2t[:, :],
                                         in_=xt[:, tt * DM:(tt + 1) * DM],
                                         func=AF.Square), "ax2", 1)

                    @block.vector
                    def _(v, c=c, tt=tt):
                        o = tt * 4
                        xv = xt[:, tt * DM:(tt + 1) * DM]
                        v.wait_ge(sem("ldx"), (4 * c + tt + 1) * 16)
                        v.reduce_sum(out=stt[:, o:o + 1], in_=xv, axis=AX.X)
                        v.wait_ge(sem("ax2"), 4 * c + tt + 1)
                        v.reduce_sum(out=stt[:, o + 2:o + 3], in_=x2t[:, :],
                                     axis=AX.X)
                        v.tensor_scalar_mul(out=stt[:, o + 1:o + 2],
                                            in0=stt[:, o:o + 1],
                                            scalar1=-1.0 / DM)
                        v.tensor_scalar_mul(out=stt[:, o:o + 1],
                                            in0=stt[:, o + 2:o + 3],
                                            scalar1=1.0 / DM)
                        v.tensor_mul(out=stt[:, o + 2:o + 3],
                                     in0=stt[:, o + 1:o + 2],
                                     in1=stt[:, o + 1:o + 2])
                        v.tensor_scalar_mul(out=stt[:, o + 3:o + 4],
                                            in0=stt[:, o + 1:o + 2], scalar1=1.0)
                        inc(v.tensor_sub(out=stt[:, o + 2:o + 3],
                                         in0=stt[:, o:o + 1],
                                         in1=stt[:, o + 2:o + 3]), "vvar", 1)

                    @block.scalar
                    def _(a, c=c, tt=tt):
                        o = tt * 4
                        if c == 0 and tt == 0:
                            W(a, "init")
                        a.wait_ge(sem("vvar"), 4 * c + tt + 1)
                        inc(a.activation(out=stt[:, o + 3:o + 4],
                                         in_=stt[:, o + 2:o + 3],
                                         func=AF.Sqrt, bias=epsv[:, 0:1]), "arstd", 1)

                    @block.vector
                    def _(v, c=c, tt=tt):
                        o = tt * 4
                        xv = xt[:, tt * DM:(tt + 1) * DM]
                        v.wait_ge(sem("arstd"), 4 * c + tt + 1)
                        inc(v.reciprocal(out=stt[:, o + 3:o + 4],
                                         in_=stt[:, o + 3:o + 4]), "vrc", 1)
                        v.wait_ge(sem("vrc"), N["vrc"])
                        v.tensor_scalar_add(out=xv, in0=xv,
                                            scalar1=stt[:, o + 1:o + 2])
                        inc(v.tensor_scalar_mul(out=xv, in0=xv,
                                                scalar1=stt[:, o + 3:o + 4]),
                            "vxh", 1)

                if S1PART < 2:
                    continue
                @block.tensor
                def _(t, c=c):
                    if c == 0:
                        W(t, "modok")
                    for tt in range(4):
                        t.wait_ge(sem("vxh"), 4 * c + tt + 1)
                        for jj in range(8):
                            idx = tt * 8 + jj
                            g = 32 * c + idx
                            if g >= 2:
                                t.wait_ge(sem("hev"), g - 1)
                            bank = ps[2 + (idx % 2)]
                            inc(t.transpose(
                                bank[:, 0:128],
                                xt[:, tt * DM + jj * 128:tt * DM + (jj + 1) * 128],
                                eye[:, :]), "htr", 1)

                @block.scalar
                def _(a, c=c):
                    if c > 0:
                        a.wait_ge(sem("wmm"), 16 * c)
                    for tt in range(4):
                        for jj in range(8):
                            idx = tt * 8 + jj
                            a.wait_ge(sem("htr"), 32 * c + idx + 1)
                            bank = ps[2 + (idx % 2)]
                            inc(a.activation(
                                out=hch[:, jj * TC1 + tt * 128:
                                        jj * TC1 + (tt + 1) * 128],
                                in_=bank[:, 0:128],
                                func=AF.Identity,
                                scale=mod2[:, CA1 * 8 + jj:CA1 * 8 + jj + 1],
                                bias=mod2[:, SA * 8 + jj:SA * 8 + jj + 1]),
                                "hev", 1)

                if S1PART < 3:
                    continue
                @block.sync
                def _(s, c=c):
                    for ji in range(16):
                        g = 16 * c + ji
                        if g >= 2:
                            s.wait_ge(sem("wmm"), g - 1)
                        half = ji % 2
                        for kk in range(8):
                            inc(s.dma_start(
                                out=wbuf[:, half * 1024 + kk * 128:
                                         half * 1024 + (kk + 1) * 128],
                                in_=w_in[kk * 128:(kk + 1) * 128,
                                         ji * 128:(ji + 1) * 128]), "ldw", 16)

                @block.tensor
                def _(t, c=c):
                    t.wait_ge(sem("hev"), 32 * (c + 1))
                    for ji in range(16):
                        g = 16 * c + ji
                        half = ji % 2
                        t.wait_ge(sem("ldw"), (g + 1) * 8 * 16)
                        if g >= 2:
                            t.wait_ge(sem("psev"), g - 1)
                        bank = ps[4] if half == 0 else ps[5]
                        for kk in range(8):
                            mi = t.matmul(bank[:, :],
                                          wbuf[:, half * 1024 + kk * 128:
                                               half * 1024 + (kk + 1) * 128],
                                          hch[:, kk * TC1:(kk + 1) * TC1],
                                          start=(kk == 0), stop=(kk == 7))
                        inc(mi, "wmm", 1)

                @block.scalar
                def _(a, c=c):
                    if c > 0:
                        a.wait_ge(sem("vmarg"), c)
                    for ji in range(16):
                        g = 16 * c + ji
                        half = ji % 2
                        bank = ps[4] if half == 0 else ps[5]
                        a.wait_ge(sem("wmm"), g + 1)
                        inc(a.activation(out=xssm[:, ji * MG + 3:ji * MG + 3 + TC1],
                                         in_=bank[:, :], func=AF.Copy), "psev", 1)

                if S1PART < 4:
                    continue
                @block.vector
                def _(v, c=c):
                    v.wait_ge(sem("psev"), 16 * (c + 1))
                    for blk in range(16):
                        base = blk * MG
                        pr = (blk % 2) * 2 * TC1
                        if 16 * c + blk >= 2:
                            v.wait_ge(sem("acv"), 16 * c + blk - 1)
                        v.tensor_scalar_mul(out=cacc[:, pr:pr + TC1],
                                            in0=xssm[:, base:base + TC1],
                                            scalar1=convw[:, blk * 4:blk * 4 + 1])
                        for kk in range(1, 4):
                            so = pr + ((kk + 1) % 2) * TC1
                            do = pr + (kk % 2) * TC1
                            v.scalar_tensor_tensor(
                                out=cacc[:, do:do + TC1],
                                in0=xssm[:, base + kk:base + kk + TC1],
                                scalar=convw[:, blk * 4 + kk:blk * 4 + kk + 1],
                                in1=cacc[:, so:so + TC1],
                                op0=ALU.mult, op1=ALU.add)
                        inc(v.engine_nop(), "vcv", 1)
                    for blk in range(16):
                        base = blk * MG
                        v.tensor_copy(out=xssm[:, base:base + 3],
                                      in_=xssm[:, base + TC1:base + TC1 + 3])
                    inc(v.engine_nop(), "vmarg", 1)

                @block.scalar
                def _(a, c=c):
                    if c > 0:
                        a.wait_ge(sem("vu"), c)
                    for blk in range(16):
                        pr = (blk % 2) * 2 * TC1
                        a.wait_ge(sem("vcv"), 16 * c + blk + 1)
                        inc(a.activation(out=xconv[:, blk * TC1:(blk + 1) * TC1],
                                         in_=cacc[:, pr + TC1:pr + 2 * TC1],
                                         func=AF.Silu,
                                         bias=convb[:, blk:blk + 1]), "acv", 1)

                if S1PART < 5:
                    continue
                @block.tensor
                def _(t, c=c):
                    t.wait_ge(sem("acv"), 16 * (c + 1))
                    if c > 0:
                        t.wait_ge(sem("bcrp"), 64 * c)
                        t.wait_ge(sem("asev"), c)
                    for kk in range(16):
                        mi = t.matmul(ps[6][0:33, :],
                                      wxsb[:, kk * 33:(kk + 1) * 33],
                                      xconv[:, kk * TC1:(kk + 1) * TC1],
                                      start=(kk == 0), stop=(kk == 15))
                    inc(mi, "wxmm", 1)
                    t.wait_ge(sem("asev"), c + 1)
                    if c > 0:
                        t.wait_ge(sem("adt"), 16 * c)
                    inc(t.matmul(ps[7][:, :], onesr[0:1, :], s_sb[0:1, :],
                                 start=True, stop=True), "sbc", 1)

                @block.scalar
                def _(a, c=c):
                    a.wait_ge(sem("wxmm"), c + 1)
                    if c > 0:
                        a.wait_ge(sem("bcrp"), 64 * c)
                    inc(a.activation(out=bc32[:, :], in_=ps[6][0:32, :],
                                     func=AF.Copy), "bcev", 1)
                    inc(a.activation(out=s_sb[0:1, :], in_=ps[6][32:33, :],
                                     func=AF.Copy), "asev", 1)
                    a.wait_ge(sem("sbc"), c + 1)
                    if c > 0:
                        a.wait_ge(sem("sp1"), 48 * c)
                        a.wait_ge(sem("vu"), c)
                    for blk in range(16):
                        mi = a.activation(out=dt_c[:, blk * TC1:(blk + 1) * TC1],
                                          in_=ps[7][:, :], func=AF.Abs,
                                          bias=beta[:, blk:blk + 1])
                    inc(mi, "dtp1", 1)
                    W(a, "dtp1")
                    for blk in range(16):
                        mi = a.activation(out=dt_c[:, blk * TC1:(blk + 1) * TC1],
                                          in_=dt_c[:, blk * TC1:(blk + 1) * TC1],
                                          func=AF.Exp, scale=-1.0)
                    inc(mi, "dtp2", 1)
                    W(a, "dtp2")
                    for blk in range(16):
                        mi = a.activation(out=dt_c[:, blk * TC1:(blk + 1) * TC1],
                                          in_=dt_c[:, blk * TC1:(blk + 1) * TC1],
                                          func=AF.Ln, bias=onep[:, 0:1])
                    inc(mi, "dtp3", 1)
                    W(a, "dtp3")
                    for blk in range(16):
                        inc(a.activation(out=u_c[:, blk * TC1:(blk + 1) * TC1],
                                         in_=ps[7][:, :], func=AF.Relu,
                                         bias=beta[:, blk:blk + 1]), "adt", 1)

                @block.sync
                def _(s, c=c):
                    s.wait_ge(sem("bcev"), c + 1)
                    if c > 0:
                        s.wait_ge(sem("sp1"), 48 * c)
                    for qd in range(4):
                        inc(s.dma_start(out=bcrep[qd * 32:(qd + 1) * 32, :],
                                        in_=bc32[:, :]), "bcrp", 16)

                if S1PART < 6:
                    continue
                @block.vector
                def _(v, c=c):
                    v.wait_ge(sem("adt"), 16 * (c + 1))
                    if c > 0:
                        v.wait_ge(sem("sp1"), 48 * c)
                    for blk in range(16):
                        v.tensor_add(out=dt_c[:, blk * TC1:(blk + 1) * TC1],
                                     in0=dt_c[:, blk * TC1:(blk + 1) * TC1],
                                     in1=u_c[:, blk * TC1:(blk + 1) * TC1])
                    for blk in range(16):
                        v.tensor_mul(out=u_c[:, blk * TC1:(blk + 1) * TC1],
                                     in0=dt_c[:, blk * TC1:(blk + 1) * TC1],
                                     in1=xconv[:, blk * TC1:(blk + 1) * TC1])
                    inc(v.engine_nop(), "vu", 1)

                @block.sync
                def _(s, c=c):
                    s.wait_ge(sem("vu"), c + 1)
                    s.wait_ge(sem("bcrp"), 64 * (c + 1))
                    inc(s.dma_start(
                        out=dtbuf[:, :].rearrange("p (b t) -> p b t", b=16)
                        [:, :, c * TC1:(c + 1) * TC1],
                        in_=dt_c[:, :].rearrange("p (b t) -> p b t", b=16)),
                        "sp1", 16)
                    inc(s.dma_start(
                        out=ubuf[:, :].rearrange("p (b t) -> p b t", b=16)
                        [:, :, c * TC1:(c + 1) * TC1],
                        in_=u_c[:, :].rearrange("p (b t) -> p b t", b=16)),
                        "sp1", 16)
                    inc(s.dma_start(out=bcbuf[:, c * TC1:(c + 1) * TC1],
                                    in_=bcrep[:, :]), "sp1", 16)
                    if debug and c == 1:
                        inc(s.dma_start(out=dbg["mod2"], in_=mod2[:, :]),
                            "dbgX", 16)
                        inc(s.dma_start(out=dbg["hch"], in_=hch[:, :]),
                            "dbgX", 16)
                        inc(s.dma_start(out=dbg["xconv"], in_=xconv[:, :]),
                            "dbgX", 16)
                        inc(s.dma_start(out=dbg["dt"], in_=dt_c[:, :]), "dbgX", 16)
                        inc(s.dma_start(out=dbg["bc"], in_=bcrep[:, :]),
                            "dbgX", 16)

        if stages < 2:
            @block.sync
            def _(s):
                nch = min(S1CH, NCH1)
                if S1PART >= 6:
                    s.wait_ge(sem("sp1"), 48 * nch)
                elif S1PART >= 5:
                    s.wait_ge(sem("adt"), 16 * nch)
                    s.wait_ge(sem("bcrp"), 64 * nch)
                elif S1PART >= 4:
                    s.wait_ge(sem("acv"), 16 * nch)
                elif S1PART >= 3:
                    s.wait_ge(sem("psev"), 16 * nch)
                elif S1PART >= 2:
                    s.wait_ge(sem("hev"), 32 * nch)
                else:
                    s.wait_ge(sem("vxh"), 4 * nch)
                if debug and "dbgX" in k.n:
                    W(s, "dbgX")
            return nc
        CH4 = [(0, 512), (512, 512), (1024, 512), (1536, 512), (2048, 5)]
        with nc.sbuf_tensor("xt4", [128, 4 * DM], F32) as xt, \
             nc.sbuf_tensor("x2t4", [128, DM], F32) as x2t, \
             nc.sbuf_tensor("stt4", [128, 16], F32) as stt, \
             nc.sbuf_tensor("hch4", [128, 8 * TC1], F32) as hch, \
             nc.sbuf_tensor("wbuf4a", [128, 2 * 8 * 128], F32) as wbuf, \
             nc.sbuf_tensor("xssm4", [128, 16 * MG], F32) as xssm, \
             nc.sbuf_tensor("xconv4", [128, 16 * TC1], F32) as xconv, \
             nc.sbuf_tensor("cacc4", [128, 4 * TC1], F32) as cacc:

            @block.vector
            def _(v):
                v.wait_ge(sem("sp1"), 48 * NCH1)
                for blk in range(16):
                    v.memset(xssm[:, blk * MG:blk * MG + 3], 0.0)

            for ci, (r0, wdt) in enumerate(CH4):
                ntt = -(-wdt // 128)
                @block.sync
                def _(s, ci=ci, r0=r0, wdt=wdt, ntt=ntt):
                    if ci == 0:
                        s.wait_ge(sem("sp1"), 48 * NCH1)
                    if ci > 0:
                        s.wait_ge(sem("htr4"), N["htr4"])
                    for tt in range(ntt):
                        nr = min(128, wdt - tt * 128)
                        inc(s.dma_start(out=xt[0:nr, tt * DM:(tt + 1) * DM],
                                        in_=x_q[r0 + tt * 128:r0 + tt * 128 + nr, :]),
                            "ldx4", 16)

                for tt in range(ntt):
                    nr = min(128, wdt - tt * 128)

                    @block.scalar
                    def _(a, ci=ci, tt=tt, nr=nr):
                        a.wait_ge(sem("ldx4"), N["ldx4"] - 16 * (ntt - tt - 1))
                        if N["ax24"] > 0:
                            a.wait_ge(sem("vvar4"), N["ax24"])
                        inc(a.activation(out=x2t[0:nr, :],
                                         in_=xt[0:nr, tt * DM:(tt + 1) * DM],
                                         func=AF.Square), "ax24", 1)

                    @block.vector
                    def _(v, ci=ci, tt=tt, nr=nr):
                        o = tt * 4
                        xv = xt[0:nr, tt * DM:(tt + 1) * DM]
                        W(v, "ax24")
                        v.reduce_sum(out=stt[0:nr, o:o + 1], in_=xv, axis=AX.X)
                        v.reduce_sum(out=stt[0:nr, o + 2:o + 3], in_=x2t[0:nr, :],
                                     axis=AX.X)
                        v.tensor_scalar_mul(out=stt[0:nr, o + 1:o + 2],
                                            in0=stt[0:nr, o:o + 1],
                                            scalar1=-1.0 / DM)
                        v.tensor_scalar_mul(out=stt[0:nr, o:o + 1],
                                            in0=stt[0:nr, o + 2:o + 3],
                                            scalar1=1.0 / DM)
                        v.tensor_mul(out=stt[0:nr, o + 2:o + 3],
                                     in0=stt[0:nr, o + 1:o + 2],
                                     in1=stt[0:nr, o + 1:o + 2])
                        v.tensor_scalar_mul(out=stt[0:nr, o + 3:o + 4],
                                            in0=stt[0:nr, o + 1:o + 2],
                                            scalar1=1.0)
                        inc(v.tensor_sub(out=stt[0:nr, o + 2:o + 3],
                                         in0=stt[0:nr, o:o + 1],
                                         in1=stt[0:nr, o + 2:o + 3]), "vvar4", 1)

                    @block.scalar
                    def _(a, tt=tt, nr=nr):
                        o = tt * 4
                        W(a, "vvar4")
                        inc(a.activation(out=stt[0:nr, o + 3:o + 4],
                                         in_=stt[0:nr, o + 2:o + 3],
                                         func=AF.Sqrt, bias=epsv[0:nr, 0:1]), "arstd4", 1)

                    @block.vector
                    def _(v, tt=tt, nr=nr):
                        o = tt * 4
                        xv = xt[0:nr, tt * DM:(tt + 1) * DM]
                        W(v, "arstd4")
                        inc(v.reciprocal(out=stt[0:nr, o + 3:o + 4],
                                         in_=stt[0:nr, o + 3:o + 4]), "vrc4", 1)
                        v.wait_ge(sem("vrc4"), N["vrc4"])
                        v.tensor_scalar_add(out=xv, in0=xv,
                                            scalar1=stt[0:nr, o + 1:o + 2])
                        inc(v.tensor_scalar_mul(out=xv, in0=xv,
                                                scalar1=stt[0:nr, o + 3:o + 4]),
                            "vxh4", 1)

                @block.tensor
                def _(t, ci=ci, wdt=wdt, ntt=ntt):
                    W(t, "vxh4")
                    for tt in range(ntt):
                        nr = min(128, wdt - tt * 128)
                        for jj in range(8):
                            idx = tt * 8 + jj
                            if N["htr4"] >= 2:
                                t.wait_ge(sem("hev4"), N["htr4"] - 1)
                            bank = ps[2 + (N["htr4"] % 2)]
                            inc(t.transpose(
                                bank[:, 0:nr],
                                xt[0:nr, tt * DM + jj * 128:tt * DM + (jj + 1) * 128],
                                eye[0:nr, 0:nr]), "htr4", 1)

                @block.scalar
                def _(a, ci=ci, wdt=wdt, ntt=ntt):
                    if ci > 0:
                        a.wait_ge(sem("wmm4"), N["wmm4"])
                    base = N["htr4"] - 8 * ntt
                    for tt in range(ntt):
                        nr = min(128, wdt - tt * 128)
                        for jj in range(8):
                            idx = tt * 8 + jj
                            a.wait_ge(sem("htr4"), base + idx + 1)
                            bank = ps[2 + ((base + idx) % 2)]
                            inc(a.activation(
                                out=hch[:, jj * TC1 + tt * 128:
                                        jj * TC1 + tt * 128 + nr],
                                in_=bank[:, 0:nr],
                                func=AF.Identity,
                                scale=mod2[:, CA1 * 8 + jj:CA1 * 8 + jj + 1],
                                bias=mod2[:, SA * 8 + jj:SA * 8 + jj + 1]),
                                "hev4", 1)

                @block.sync
                def _(s, ci=ci):
                    for ji in range(32):
                        g = 32 * ci + ji
                        if g >= 2:
                            s.wait_ge(sem("wmm4"), g - 1)
                        half = ji % 2
                        col = ji * 128 if ji < 16 else DI + (ji - 16) * 128
                        for kk in range(8):
                            inc(s.dma_start(
                                out=wbuf[:, half * 1024 + kk * 128:
                                         half * 1024 + (kk + 1) * 128],
                                in_=w_in[kk * 128:(kk + 1) * 128, col:col + 128]),
                                "ldw4", 16)

                @block.tensor
                def _(t, ci=ci, wdt=wdt):
                    W(t, "hev4")
                    for ji in range(32):
                        g = 32 * ci + ji
                        half = ji % 2
                        t.wait_ge(sem("ldw4"), (g + 1) * 8 * 16)
                        if g >= 2:
                            t.wait_ge(sem("psev4"), g - 1)
                        bank = ps[4] if half == 0 else ps[5]
                        for kk in range(8):
                            mi = t.matmul(bank[:, 0:wdt],
                                          wbuf[:, half * 1024 + kk * 128:
                                               half * 1024 + (kk + 1) * 128],
                                          hch[:, kk * TC1:kk * TC1 + wdt],
                                          start=(kk == 0), stop=(kk == 7))
                        inc(mi, "wmm4", 1)

                @block.scalar
                def _(a, ci=ci, r0=r0, wdt=wdt):
                    if ci > 0:
                        a.wait_ge(sem("vmarg4"), ci)
                    base = 32 * ci
                    for ji in range(32):
                        half = ji % 2
                        bank = ps[4] if half == 0 else ps[5]
                        a.wait_ge(sem("wmm4"), base + ji + 1)
                        if ji < 16:
                            inc(a.activation(
                                out=xssm[:, ji * MG + 3:ji * MG + 3 + wdt],
                                in_=bank[:, 0:wdt], func=AF.Copy), "psev4", 1)
                        else:
                            blk = ji - 16
                            if ji == 16:
                                a.wait_ge(sem("xcst4"), N["xcst4"])
                            inc(a.activation(
                                out=xconv[:, blk * TC1:blk * TC1 + wdt],
                                in_=bank[:, 0:wdt], func=AF.Silu), "psev4", 1)

                @block.sync
                def _(s, ci=ci, r0=r0, wdt=wdt):
                    W(s, "psev4")
                    lo = max(0, 4 - r0)
                    npos = wdt - lo
                    if npos > 0:
                        p0 = r0 - 4 + lo
                        inc(s.dma_start(
                            out=zq[:, :].rearrange("p (b t) -> p b t", b=16)
                            [:, :, p0:p0 + npos],
                            in_=xconv[:, :].rearrange("p (b t) -> p b t", b=16)
                            [:, :, lo:lo + npos]), "zst4", 16)

                if ci == 0:
                    @block.vector
                    def _(v):
                        W(v, "psev4")
                        v.tensor_scalar_mul(
                            out=xssm[:, :].rearrange("p (b t) -> p b t", b=16)
                            [:, :, 3:8],
                            in0=xssm[:, :].rearrange("p (b t) -> p b t", b=16)
                            [:, :, 3:8],
                            scalar1=halosc[:, 0:1])
                        v.tensor_scalar_mul(out=cacc[:, 0:1], in0=cacc[:, 0:1],
                                            scalar1=1.0)

                @block.vector
                def _(v, ci=ci, r0=r0, wdt=wdt):

                    v.wait_ge(sem("zst4"), N["zst4"])
                    v.wait_ge(sem("psev4"), N["psev4"])
                    for blk in range(16):
                        base = blk * MG
                        pr = (blk % 2) * 2 * TC1
                        if N["vcv4"] >= 2:
                            v.wait_ge(sem("acv4"), N["vcv4"] - 1)
                        v.tensor_scalar_mul(out=cacc[:, pr:pr + wdt],
                                            in0=xssm[:, base:base + wdt],
                                            scalar1=convw[:, blk * 4:blk * 4 + 1])
                        for kk in range(1, 4):
                            so = pr + ((kk + 1) % 2) * TC1
                            do = pr + (kk % 2) * TC1
                            v.scalar_tensor_tensor(
                                out=cacc[:, do:do + wdt],
                                in0=xssm[:, base + kk:base + kk + wdt],
                                scalar=convw[:, blk * 4 + kk:blk * 4 + kk + 1],
                                in1=cacc[:, so:so + wdt],
                                op0=ALU.mult, op1=ALU.add)
                        inc(v.engine_nop(), "vcv4", 1)
                    for blk in range(16):
                        base = blk * MG
                        v.tensor_copy(out=xssm[:, base:base + 3],
                                      in_=xssm[:, base + wdt:base + wdt + 3])
                    inc(v.engine_nop(), "vmarg4", 1)

                @block.scalar
                def _(a, ci=ci, wdt=wdt):
                    base = N["acv4"]
                    for blk in range(16):
                        pr = (blk % 2) * 2 * TC1
                        a.wait_ge(sem("vcv4"), base + blk + 1)
                        if blk == 0:
                            a.wait_ge(sem("xcst4"), N["xcst4"])
                        inc(a.activation(out=xconv[:, blk * TC1:blk * TC1 + wdt],
                                         in_=cacc[:, pr + TC1:pr + TC1 + wdt],
                                         func=AF.Silu,
                                         bias=convb[:, blk:blk + 1]), "acv4", 1)

                @block.sync
                def _(s, ci=ci, r0=r0, wdt=wdt):
                    W(s, "acv4")
                    lo = max(0, 4 - r0)
                    npos = wdt - lo
                    if npos > 0:
                        p0 = r0 - 4 + lo
                        inc(s.dma_start(
                            out=xcq[:, :].rearrange("p (b t) -> p b t", b=16)
                            [:, :, p0:p0 + npos],
                            in_=xconv[:, :].rearrange("p (b t) -> p b t", b=16)
                            [:, :, lo:lo + npos]), "xcst4", 16)

            if debug:
                @block.sync
                def _(s):
                    W(s, "xcst4")
                    W(s, "zst4")
                    inc(s.dma_start(out=dbg["xcq"], in_=xcq[:, :]), "dbgX", 16)
                    inc(s.dma_start(out=dbg["zq"], in_=zq[:, :]), "dbgX", 16)

        if stages < 3:
            @block.sync
            def _(s):
                W(s, "xcst4")
                W(s, "zst4")
                if debug:
                    for tg in ("dbgX",):
                        if tg in k.n:
                            W(s, tg)
            return nc
        with nc.sbuf_tensor("dt2", [128, 8 * TC2], F32) as dt2, \
             nc.sbuf_tensor("u2", [128, 8 * TC2], F32) as u2, \
             nc.sbuf_tensor("bcr2", [128, TC2], F32) as bcr2, \
             nc.sbuf_tensor("bb", [128, TC2], F32) as bb, \
             nc.sbuf_tensor("cc_", [128, TC2], F32) as cc_, \
             nc.sbuf_tensor("rr", [128, 2 * TC2], F32) as rr, \
             nc.sbuf_tensor("ww", [128, TC2], F32) as ww, \
             nc.sbuf_tensor("hh", [128, TC2], F32) as hh, \
             nc.sbuf_tensor("mm_", [128, TC2], F32) as mm_, \
             nc.sbuf_tensor("yall", [128, 8 * TC2], F32) as yall, \
             nc.sbuf_tensor("ytok", [128, 2 * 1024], F32) as ytok:

            @block.vector
            def _(v):
                inc(v.memset(ytok[0:1, :], 0.0), "zrv", 1)

            @block.sync
            def _(s):
                W(s, "zrv")
                inc(s.dma_start(out=ystage[0:1, :], in_=ytok[0:1, 0:DI]), "zr", 16)

            nscan = 0
            for c2 in range(NCH2):
                for wv in range(2):
                    cw = c2 * 2 + wv

                    @block.sync
                    def _(s, c2=c2, wv=wv, cw=cw):
                        if cw == 0:
                            s.wait_ge(sem("sp1"), 48 * NCH1)
                            s.wait_ge(sem("xcst4"), N["xcst4"])
                            s.wait_ge(sem("zst4"), N["zst4"])
                            s.wait_ge(sem("acv4"), N["acv4"])
                            s.wait_ge(sem("vmarg4"), N["vmarg4"])
                        else:
                            s.wait_ge(sem("ract"), 128 * cw)
                            s.wait_ge(sem("ydone"), cw)
                            if debug and "dbgS" in k.n:
                                s.wait_ge(sem("dbgS"), N["dbgS"])
                        src = dtbuf[:, :].rearrange("p (b t) -> p b t", b=16)
                        inc(s.dma_start(
                            out=dt2[:, :].rearrange("p (b t) -> p b t", b=8),
                            in_=src[:, wv * 8:wv * 8 + 8, c2 * TC2:(c2 + 1) * TC2]),
                            "s2ld", 16)
                        src = ubuf[:, :].rearrange("p (b t) -> p b t", b=16)
                        inc(s.dma_start(
                            out=u2[:, :].rearrange("p (b t) -> p b t", b=8),
                            in_=src[:, wv * 8:wv * 8 + 8, c2 * TC2:(c2 + 1) * TC2]),
                            "s2ld", 16)
                        if wv == 0:
                            if c2 > 0:
                                s.wait_ge(sem("ydone"), cw)
                            inc(s.dma_start(out=bcr2[:, :],
                                            in_=bcbuf[:, c2 * TC2:(c2 + 1) * TC2]),
                                "s2ldb", 16)
                        if debug and cw <= 1:
                            s.wait_ge(sem("s2ld"), 32 * (cw + 1))
                            if cw == 0:
                                s.wait_ge(sem("s2ldb"), 16)
                                inc(s.dma_start(out=dbg["bcr2c0"],
                                                in_=bcr2[:, :]), "dbgS", 16)
                            inc(s.dma_start(out=dbg[f"dt2w{cw}"],
                                            in_=dt2[:, :]), "dbgS", 16)
                            inc(s.dma_start(out=dbg[f"u2w{cw}"],
                                            in_=u2[:, :]), "dbgS", 16)

                    @block.vector
                    def _(v, c2=c2, wv=wv, cw=cw, nscan=nscan):
                        v.wait_ge(sem("s2ld"), 32 * (cw + 1))
                        v.wait_ge(sem("s2ldb"), 16 * (c2 + 1))
                        if cw > 0:
                            v.wait_ge(sem("yttr"), 64 * cw)
                        for n in range(NLOC):
                            v.stream_shuffle(out=bb[:, :], in_=bcr2[:, :],
                                             mask=[n] * 32)
                            v.stream_shuffle(out=cc_[:, :], in_=bcr2[:, :],
                                             mask=[16 + n] * 32)
                            for bl in range(8):
                                it = nscan + n * 8 + bl
                                col = (wv * 8 + bl) * NLOC + n
                                v.wait_ge(sem("ract"), it + 1)
                                v.tensor_mul(out=ww[:, :],
                                             in0=u2[:, bl * TC2:(bl + 1) * TC2],
                                             in1=bb[:, :])
                                v.tensor_tensor_scan(
                                    out=hh[:, :],
                                    data0=rr[:, (it % 2) * TC2:(it % 2 + 1) * TC2],
                                    data1=ww[:, :],
                                    initial=states[:, col:col + 1],
                                    op0=ALU.mult, op1=ALU.add)
                                inc(v.engine_nop(), "vscan", 1)
                                v.tensor_mul(out=mm_[:, :], in0=hh[:, :],
                                             in1=cc_[:, :])
                                v.tensor_copy(out=states[:, col:col + 1],
                                              in_=hh[:, TC2 - 1:TC2])
                                if n == 0:
                                    v.tensor_copy(
                                        out=yall[:, bl * TC2:(bl + 1) * TC2],
                                        in_=mm_[:, :])
                                else:
                                    v.tensor_add(
                                        out=yall[:, bl * TC2:(bl + 1) * TC2],
                                        in0=yall[:, bl * TC2:(bl + 1) * TC2],
                                        in1=mm_[:, :])
                        inc(v.engine_nop(), "ydone", 1)

                    @block.scalar
                    def _(a, c2=c2, wv=wv, cw=cw, nscan=nscan):
                        a.wait_ge(sem("s2ld"), 32 * (cw + 1))
                        if cw == 0:
                            W(a, "anegd")
                        for n in range(NLOC):
                            for bl in range(8):
                                it = nscan + n * 8 + bl
                                if it >= 2:
                                    a.wait_ge(sem("vscan"), it - 1)
                                inc(a.activation(
                                    out=rr[:, (it % 2) * TC2:(it % 2 + 1) * TC2],
                                    in_=dt2[:, bl * TC2:(bl + 1) * TC2],
                                    func=AF.Exp,
                                    scale=aneg[:, (wv * 8 + bl) * NLOC + n:
                                               (wv * 8 + bl) * NLOC + n + 1]),
                                    "ract", 1)

                    nscan += NLOC * 8

                    @block.tensor
                    def _(t, cw=cw):
                        t.wait_ge(sem("ydone"), cw + 1)
                        for tt in range(8):
                            for bl in range(8):
                                idx = tt * 8 + bl
                                g = 64 * cw + idx
                                if g >= 2:
                                    t.wait_ge(sem("ytev"), g - 1)
                                bank = ps[2 + (g % 2)]
                                inc(t.transpose(
                                    bank[:, 0:128],
                                    yall[:, bl * TC2 + tt * 128:
                                         bl * TC2 + (tt + 1) * 128],
                                    eye[:, :]), "yttr", 1)

                    @block.scalar
                    def _(a, cw=cw):
                        if cw == 0:
                            W(a, "zr")
                        for tt in range(8):
                            half = tt % 2
                            if cw * 8 + tt >= 2:
                                a.wait_ge(sem("yst"), 16 * (cw * 8 + tt - 1))
                                a.wait_ge(sem("ystx"), N["ystx"])
                            for bl in range(8):
                                idx = tt * 8 + bl
                                g = 64 * cw + idx
                                a.wait_ge(sem("yttr"), g + 1)
                                bank = ps[2 + (g % 2)]
                                inc(a.activation(
                                    out=ytok[:, half * 1024 + bl * 128:
                                             half * 1024 + (bl + 1) * 128],
                                    in_=bank[:, 0:128],
                                    func=AF.Copy), "ytev", 1)

                    @block.sync
                    def _(s, c2=c2, wv=wv, cw=cw):
                        for tt in range(8):
                            half = tt % 2
                            s.wait_ge(sem("ytev"), 64 * cw + (tt + 1) * 8)
                            tg = c2 * TC2 + tt * 128
                            qq = tg // QT
                            row = qq * HQ + 1 + (tg % QT)
                            inc(s.dma_start(
                                out=ystage[row:row + 128,
                                           wv * 1024:(wv + 1) * 1024],
                                in_=ytok[:, half * 1024:(half + 1) * 1024]),
                                "yst", 16)
                            if (tg + 128) % QT == 0 and (tg + 128) < T:
                                q2 = (tg + 128) // QT
                                inc(s.dma_start(
                                    out=ystage[q2 * HQ:q2 * HQ + 1,
                                               wv * 1024:(wv + 1) * 1024],
                                    in_=ytok[127:128, half * 1024:(half + 1) * 1024]),
                                    "ystx", 16)

            NYST = N["yst"]
            NYSTX = N["ystx"]

        if stages < 4:
            @block.sync
            def _(s):
                s.wait_ge(sem("yst"), NYST)
                s.wait_ge(sem("ystx"), NYSTX)
                if debug:
                    for tg in ("dbgX",):
                        if tg in k.n:
                            W(s, tg)
            return nc
        @block.gpsimd
        def _(g):
            g.wait_ge(sem("yst"), NYST)
            g.wait_ge(sem("ystx"), NYSTX)
            W(g, "zr")
            inc(g.collective_compute(
                "ReduceScatter", ALU.add, ins=[ystage[:, :]], outs=[yq[:, :]],
                replica_groups=[[0, 1, 2, 3], [4, 5, 6, 7]]), "rs", 1)

        if debug:
            @block.sync
            def _(s):
                s.wait_ge(sem("yst"), NYST)
                s.wait_ge(sem("ystx"), NYSTX)
                W(s, "zr")
                inc(s.dma_start(out=dbg["ystage"], in_=ystage[:, :]), "dbgX", 16)
                W(s, "rs")
                inc(s.dma_start(out=dbg["yq"], in_=yq[:, :]), "dbgX", 16)

        if stages < 5:
            @block.sync
            def _(s):
                W(s, "rs")
                if debug:
                    for tg in ("dbgX",):
                        if tg in k.n:
                            W(s, tg)
            return nc
        with nc.sbuf_tensor("ytk4", [128, 2 * DI], F32) as ytk4, \
             nc.sbuf_tensor("yfm", [128, 16 * 513], F32) as yfm, \
             nc.sbuf_tensor("xzt", [128, 2 * 2 * 513], F32) as xzt, \
             nc.sbuf_tensor("xtk", [128, 2 * DM], F32) as xtk, \
             nc.sbuf_tensor("xfm", [128, 8 * 513], F32) as xfm, \
             nc.sbuf_tensor("xmid", [128, 8 * 513], F32) as xmid, \
             nc.sbuf_tensor("h2", [128, 8 * 513], F32) as h2, \
             nc.sbuf_tensor("sr4", [1, 1600], F32) as sr4, \
             nc.sbuf_tensor("dif", [128, 512], F32) as dif, \
             nc.sbuf_tensor("xr", [128, 8 * 513], F32) as xr, \
             nc.sbuf_tensor("xk", [128, 8 * 512], F32) as xk, \
             nc.sbuf_tensor("k2", [128, 8 * 512], F32) as k2, \
             nc.sbuf_tensor("rv", [128, 8 * 512], F32) as rv, \
             nc.sbuf_tensor("wb4", [128, 2 * 16 * 128], F32) as wb4:

            for k4 in range(min(4, S4TILES)):
                P0 = 512 * k4

                @block.sync
                def _(s, k4=k4, P0=P0):
                    if k4 == 0:
                        W(s, "rs")
                    else:
                        if S4PART >= 2:
                            s.wait_ge(sem("gat"), 16 * k4)
                        elif S4SUB >= 2:
                            s.wait_ge(sem("yev4"), 64 * k4)
                        if S4PART >= 3:
                            s.wait_ge(sem("woev"), 8 * k4)
                        elif S4SUB >= 3:
                            s.wait_ge(sem("xev4"), 32 * k4)
                    for tt in range(4):
                        half = tt % 2
                        if tt >= 2:
                            s.wait_ge(sem("ytr4"), 64 * k4 + (tt - 1) * 16)
                        inc(s.dma_start(
                            out=ytk4[:, half * DI:(half + 1) * DI],
                            in_=yq[P0 + tt * 128:P0 + (tt + 1) * 128, :]),
                            "yld4", 16)
                    if S4HALO:
                        inc(s.dma_start(
                            out=yfm[:, 512:16 * 513:513],
                            in_=yq[P0 + 512:P0 + 513, :]
                            .rearrange("r (b p) -> (r p) b", p=128),
                            allow_slow_non_contiguous=True), "yxc4", 16)
                    for tt in range(4):
                        half = tt % 2
                        if tt >= 2:
                            s.wait_ge(sem("xtr4"), 32 * k4 + (tt - 1) * 8)
                        inc(s.dma_start(
                            out=xtk[:, half * DM:(half + 1) * DM],
                            in_=x_q[4 + P0 + tt * 128:4 + P0 + (tt + 1) * 128, :]),
                            "xld4b", 16)
                    if S4HALO:
                        inc(s.dma_start(
                            out=xfm[:, 512:8 * 513:513],
                            in_=x_q[4 + P0 + 512:4 + P0 + 513, :]
                            .rearrange("r (b p) -> (r p) b", p=128),
                            allow_slow_non_contiguous=True), "xxc4", 16)

                @block.tensor
                def _(t, k4=k4):
                    if k4 > 0:
                        if S4PART >= 6:
                            t.wait_ge(sem("oev"), 32 * k4)
                        if S4PART >= 4:
                            t.wait_ge(sem("bcev4"), k4)
                    if S4SUB < 2:
                        return
                    for tt in range(4):
                        half = tt % 2
                        t.wait_ge(sem("yld4"), 64 * k4 + (tt + 1) * 16)
                        for bl in range(16):
                            idx = tt * 16 + bl
                            if idx >= 2:
                                t.wait_ge(sem("yev4"), 64 * k4 + idx - 1)
                            bank = ps[2 + (idx % 2)]
                            inc(t.transpose(
                                bank[:, 0:128],
                                ytk4[:, half * DI + bl * 128:
                                     half * DI + (bl + 1) * 128],
                                eye[:, :]), "ytr4", 1)
                    if S4SUB < 3:
                        return
                    for tt in range(4):
                        half = tt % 2
                        t.wait_ge(sem("xld4b"), 64 * k4 + (tt + 1) * 16)
                        for jj in range(8):
                            idx = tt * 8 + jj
                            if idx >= 2:
                                t.wait_ge(sem("xev4"), 32 * k4 + idx - 1)
                            bank = ps[6] if idx % 2 == 0 else ps[7]
                            inc(t.transpose(
                                bank[:, 0:128],
                                xtk[:, half * DM + jj * 128:
                                    half * DM + (jj + 1) * 128],
                                eye[:, :]), "xtr4", 1)

                @block.scalar
                def _(a, k4=k4):
                    if S4SUB < 2:
                        return
                    for tt in range(4):
                        for bl in range(16):
                            idx = tt * 16 + bl
                            a.wait_ge(sem("ytr4"), 64 * k4 + idx + 1)
                            bank = ps[2 + (idx % 2)]
                            inc(a.activation(
                                out=yfm[:, bl * 513 + tt * 128:
                                        bl * 513 + (tt + 1) * 128],
                                in_=bank[:, 0:128],
                                func=AF.Copy), "yev4", 1)
                    if S4SUB < 3:
                        return
                    for tt in range(4):
                        for jj in range(8):
                            idx = tt * 8 + jj
                            a.wait_ge(sem("xtr4"), 32 * k4 + idx + 1)
                            bank = ps[6] if idx % 2 == 0 else ps[7]
                            inc(a.activation(
                                out=xfm[:, jj * 513 + tt * 128:
                                        jj * 513 + (tt + 1) * 128],
                                in_=bank[:, 0:128],
                                func=AF.Copy), "xev4", 1)

                if S4PART < 2:
                    continue
                @block.sync
                def _(s, k4=k4, P0=P0):
                    for bl in range(16):
                        half = bl % 2
                        if bl >= 2:
                            s.wait_ge(sem("gat"), 16 * k4 + bl - 1)
                        inc(s.dma_start(
                            out=xzt[:, half * 1026:half * 1026 + 513],
                            in_=xcq[:, bl * NPOS + P0:bl * NPOS + P0 + 513]),
                            "xzl4", 16)
                        inc(s.dma_start(
                            out=xzt[:, half * 1026 + 513:half * 1026 + 1026],
                            in_=zq[:, bl * NPOS + P0:bl * NPOS + P0 + 513]),
                            "xzl4", 16)

                @block.vector
                def _(v, k4=k4):
                    v.wait_ge(sem("yev4"), N["yev4"])
                    v.wait_ge(sem("yxc4"), N["yxc4"])
                    base = N["xzl4"] - 32 * 16
                    for bl in range(16):
                        half = bl % 2
                        v.wait_ge(sem("xzl4"), base + 32 * (bl + 1))
                        v.scalar_tensor_tensor(
                            out=yfm[:, bl * 513:(bl + 1) * 513],
                            in0=xzt[:, half * 1026:half * 1026 + 513],
                            scalar=dvec[:, bl:bl + 1],
                            in1=yfm[:, bl * 513:(bl + 1) * 513],
                            op0=ALU.mult, op1=ALU.add)
                        v.tensor_mul(
                            out=yfm[:, bl * 513:(bl + 1) * 513],
                            in0=yfm[:, bl * 513:(bl + 1) * 513],
                            in1=xzt[:, half * 1026 + 513:half * 1026 + 1026])
                        inc(v.engine_nop(), "gat", 1)

                if S4PART < 3:
                    continue
                @block.sync
                def _(s, k4=k4):
                    for j in range(8):
                        g4 = N["womm"] + j
                        if j < 2 and k4 > 0:
                            s.wait_ge(sem("fmm"), N["fmm"])
                        if j >= 2 or k4 > 0:
                            s.wait_ge(sem("womm"), g4 - 1)
                        half = j % 2
                        for kk in range(16):
                            inc(s.dma_start(
                                out=wb4[:, half * DI + kk * 128:
                                        half * DI + (kk + 1) * 128],
                                in_=w_out[kk * 128:(kk + 1) * 128,
                                          j * 128:(j + 1) * 128]), "ldwo", 16)

                @block.tensor
                def _(t, k4=k4):
                    W(t, "gat")
                    t.wait_ge(sem("xev4"), 32 * (k4 + 1))
                    base = N["womm"]
                    for j in range(8):
                        half = j % 2
                        t.wait_ge(sem("ldwo"), N["ldwo"] - (7 - j) * 16 * 16)
                        if base + j >= 2:
                            t.wait_ge(sem("woev"), base + j - 1)
                        bank = ps[4] if half == 0 else ps[5]
                        for kk in range(16):
                            t.matmul(bank[:, 0:512],
                                     wb4[:, half * DI + kk * 128:
                                         half * DI + (kk + 1) * 128],
                                     yfm[:, kk * 513:kk * 513 + 512],
                                     start=(kk == 0), stop=(kk == 15))
                        bank1 = ps[6] if half == 0 else ps[7]
                        for kk in range(16):
                            mi = t.matmul(bank1[:, 0:1],
                                          wb4[:, half * DI + kk * 128:
                                              half * DI + (kk + 1) * 128],
                                          yfm[:, kk * 513 + 512:kk * 513 + 513],
                                          start=(kk == 0), stop=(kk == 15))
                        inc(mi, "womm", 1)

                @block.vector
                def _(v, k4=k4):
                    v.wait_ge(sem("xev4"), N["xev4"])
                    v.wait_ge(sem("xxc4"), N["xxc4"])
                    base = N["womm"] - 8
                    for j in range(8):
                        half = j % 2
                        bank = ps[4] if half == 0 else ps[5]
                        v.wait_ge(sem("womm"), base + j + 1)
                        v.scalar_tensor_tensor(
                            out=xmid[:, j * 513:j * 513 + 512],
                            in0=bank[:, 0:512], scalar=mod2[:, GA * 8 + j:GA * 8 + j + 1],
                            in1=xfm[:, j * 513:j * 513 + 512],
                            op0=ALU.mult, op1=ALU.add)
                        v.scalar_tensor_tensor(
                            out=xmid[:, j * 513 + 512:j * 513 + 513],
                            in0=(ps[6] if half == 0 else ps[7])[:, 0:1],
                            scalar=mod2[:, GA * 8 + j:GA * 8 + j + 1],
                            in1=xfm[:, j * 513 + 512:j * 513 + 513],
                            op0=ALU.mult, op1=ALU.add)
                        inc(v.engine_nop(), "woev", 1)

                if S4PART < 4:
                    continue
                @block.scalar
                def _(a, k4=k4):
                    W(a, "woev")
                    if k4 > 0:
                        a.wait_ge(sem("fmm"), 24 * k4)
                    inc(a.activation(out=xr[:, 0:8 * 513], in_=xmid[:, :],
                                     func=AF.Square), "sq4d", 1)

                @block.tensor
                def _(t, k4=k4):
                    W(t, "sq4d")
                    W(t, "woev")
                    if k4 > 0:
                        t.wait_ge(sem("bcev4"), N["bcev4"])
                    for (srcT, b512, b1) in ((xmid, ps[7], ps[4]),
                                             (xr, ps[6], ps[5])):
                        for kk in range(8):
                            t.matmul(b512[0:1, 0:512], onesc[:, 0:1],
                                     srcT[:, kk * 513:kk * 513 + 512],
                                     start=(kk == 0), stop=(kk == 7))
                        for kk in range(8):
                            mi = t.matmul(b1[0:1, 0:1], onesc[:, 0:1],
                                          srcT[:, kk * 513 + 512:kk * 513 + 513],
                                          start=(kk == 0), stop=(kk == 7))
                    inc(mi, "lnmm", 1)

                @block.scalar
                def _(a, k4=k4):
                    W(a, "lnmm")
                    a.activation(out=sr4[0:1, 0:512], in_=ps[7][0:1, 0:512],
                                 func=AF.Copy)
                    a.activation(out=sr4[0:1, 512:513], in_=ps[4][0:1, 0:1],
                                 func=AF.Copy)
                    a.activation(out=sr4[0:1, 513:1025], in_=ps[6][0:1, 0:512],
                                 func=AF.Copy)
                    inc(a.activation(out=sr4[0:1, 1025:1026], in_=ps[5][0:1, 0:1],
                                     func=AF.Copy), "lnev", 1)

                @block.vector
                def _(v, k4=k4):
                    W(v, "lnev")
                    v.tensor_scalar_mul(out=sr4[0:1, 1026:1539],
                                        in0=sr4[0:1, 0:513], scalar1=-1.0 / DM)
                    v.tensor_scalar_mul(out=sr4[0:1, 0:513],
                                        in0=sr4[0:1, 513:1026], scalar1=1.0 / DM)
                    v.tensor_mul(out=sr4[0:1, 513:1026], in0=sr4[0:1, 1026:1539],
                                 in1=sr4[0:1, 1026:1539])
                    inc(v.tensor_sub(out=sr4[0:1, 0:513], in0=sr4[0:1, 0:513],
                                     in1=sr4[0:1, 513:1026]), "lnvar", 1)

                @block.scalar
                def _(a, k4=k4):
                    W(a, "lnvar")
                    inc(a.activation(out=sr4[0:1, 513:1026], in_=sr4[0:1, 0:513],
                                     func=AF.Sqrt, bias=epsv[0:1, 0:1]), "lnrs", 1)

                @block.vector
                def _(v, k4=k4):
                    W(v, "lnrs")
                    inc(v.reciprocal(out=sr4[0:1, 513:1026],
                                     in_=sr4[0:1, 513:1026]), "lnrc", 1)

                @block.tensor
                def _(t, k4=k4):
                    W(t, "lnrc")
                    t.matmul(ps[2][:, 0:512], onesr[0:1, :], sr4[0:1, 1026:1538],
                             start=True, stop=True)
                    t.matmul(ps[4][:, 0:1], onesr[0:1, :], sr4[0:1, 1538:1539],
                             start=True, stop=True)
                    t.matmul(ps[3][:, 0:512], onesr[0:1, :], sr4[0:1, 513:1025],
                             start=True, stop=True)
                    inc(t.matmul(ps[5][:, 0:1], onesr[0:1, :], sr4[0:1, 1025:1026],
                                 start=True, stop=True), "bcmm", 1)

                @block.vector
                def _(v, k4=k4):
                    W(v, "bcmm")
                    if k4 > 0:
                        v.wait_ge(sem("ost"), 64 * k4)
                    for j in range(8):
                        v.tensor_add(out=h2[:, j * 513:j * 513 + 512],
                                     in0=xmid[:, j * 513:j * 513 + 512],
                                     in1=ps[2][:, 0:512])
                        v.tensor_add(out=h2[:, j * 513 + 512:j * 513 + 513],
                                     in0=xmid[:, j * 513 + 512:j * 513 + 513],
                                     in1=ps[4][:, 0:1])
                        v.tensor_mul(out=h2[:, j * 513:j * 513 + 512],
                                     in0=h2[:, j * 513:j * 513 + 512],
                                     in1=ps[3][:, 0:512])
                        v.tensor_mul(out=h2[:, j * 513 + 512:j * 513 + 513],
                                     in0=h2[:, j * 513 + 512:j * 513 + 513],
                                     in1=ps[5][:, 0:1])
                    inc(v.engine_nop(), "bcev4", 1)

                @block.scalar
                def _(a, k4=k4):
                    W(a, "bcev4")
                    for j in range(8):
                        inc(a.activation(out=h2[:, j * 513:(j + 1) * 513],
                                         in_=h2[:, j * 513:(j + 1) * 513],
                                         func=AF.Identity,
                                         scale=mod2[:, CB1 * 8 + j:CB1 * 8 + j + 1],
                                         bias=mod2[:, SB * 8 + j:SB * 8 + j + 1]),
                                         "h2d", 1)

                if k4 == 0:
                    @block.vector
                    def _(v):
                        W(v, "h2d")
                        v.tensor_scalar_mul(
                            out=h2[:, 0:8 * 513:513],
                            in0=h2[:, 0:8 * 513:513],
                            scalar1=halosc[:, 0:1])
                        inc(v.tensor_scalar_mul(out=dif[:, 0:1], in0=dif[:, 0:1],
                                                scalar1=1.0), "h2h", 1)
                else:
                    @block.vector
                    def _(v):
                        inc(v.engine_nop(), "h2h", 1)

                if debug and k4 == 0:
                    @block.sync
                    def _(s):
                        W(s, "h2h")
                        inc(s.dma_start(out=dbg["xmid"], in_=xmid[:, :]),
                            "dbgX", 16)
                        inc(s.dma_start(out=dbg["h2"], in_=h2[:, :]), "dbgX", 16)

                if S4PART < 5:
                    continue
                @block.vector
                def _(v, k4=k4):
                    W(v, "h2d")
                    W(v, "h2h")
                    for j in range(8):
                        v.tensor_sub(out=dif[:, :],
                                     in0=h2[:, j * 513 + 1:j * 513 + 513],
                                     in1=h2[:, j * 513:j * 513 + 512])
                        v.scalar_tensor_tensor(
                            out=xr[:, j * 513:j * 513 + 512],
                            in0=dif[:, :], scalar=mur[:, j:j + 1],
                            in1=h2[:, j * 513:j * 513 + 512],
                            op0=ALU.mult, op1=ALU.add)
                        v.scalar_tensor_tensor(
                            out=xk[:, j * 512:(j + 1) * 512],
                            in0=dif[:, :], scalar=muk[:, j:j + 1],
                            in1=h2[:, j * 513:j * 513 + 512],
                            op0=ALU.mult, op1=ALU.add)
                    inc(v.engine_nop(), "xrk", 1)

                for wi, (wt, rhs_, dsts) in enumerate((
                        (wrT, "xr", "rv"), (wkT, "xk", "k2"), (wvT, "k2", None))):
                    @block.sync
                    def _(s, k4=k4, wi=wi, wt=wt):
                        for j in range(8):
                            g4 = N["fmm"] + j
                            if wi == 0 and j < 2:
                                s.wait_ge(sem("womm"), N["womm"])
                            if g4 >= 2:
                                s.wait_ge(sem("fmm"), g4 - 1)
                            half = j % 2
                            for kk in range(8):
                                inc(s.dma_start(
                                    out=wb4[:, half * DI + kk * 128:
                                            half * DI + (kk + 1) * 128],
                                    in_=wt[kk * 128:(kk + 1) * 128,
                                           j * 128:(j + 1) * 128]), "ldwf", 16)

                    @block.tensor
                    def _(t, k4=k4, wi=wi, rhs_=rhs_):
                        rhs = {"xr": xr, "xk": xk, "k2": k2}[rhs_]
                        stride = 513 if rhs_ == "xr" else 512
                        if wi == 0:
                            W(t, "xrk")
                            W(t, "bcev4")
                        else:
                            W(t, "fev")
                        base = N["fmm"]
                        for j in range(8):
                            half = j % 2
                            t.wait_ge(sem("ldwf"), N["ldwf"] - (7 - j) * 8 * 16)
                            if base + j >= 2:
                                t.wait_ge(sem("fev"), base + j - 1)
                            bank = ps[4] if half == 0 else ps[5]
                            for kk in range(8):
                                mi = t.matmul(bank[:, 0:512],
                                              wb4[:, half * DI + kk * 128:
                                                  half * DI + (kk + 1) * 128],
                                              rhs[:, kk * stride:kk * stride + 512],
                                              start=(kk == 0), stop=(kk == 7))
                            inc(mi, "fmm", 1)

                    if wi == 0:
                        @block.scalar
                        def _(a, k4=k4):
                            base = N["fmm"] - 8
                            for j in range(8):
                                half = j % 2
                                bank = ps[4] if half == 0 else ps[5]
                                a.wait_ge(sem("fmm"), base + j + 1)
                                inc(a.activation(out=rv[:, j * 512:(j + 1) * 512],
                                                 in_=bank[:, 0:512],
                                                 func=AF.Sigmoid), "fev", 1)
                    elif wi == 1:
                        @block.scalar
                        def _(a, k4=k4):
                            base = N["fmm"] - 8
                            for j in range(8):
                                half = j % 2
                                bank = ps[4] if half == 0 else ps[5]
                                a.wait_ge(sem("fmm"), base + j + 1)
                                mi = a.activation(out=k2[:, j * 512:(j + 1) * 512],
                                                  in_=bank[:, 0:512], func=AF.Relu)
                                inc(mi, "krl", 1)
                                W(a, "krl")
                                inc(a.activation(out=k2[:, j * 512:(j + 1) * 512],
                                                 in_=k2[:, j * 512:(j + 1) * 512],
                                                 func=AF.Square), "fev", 1)
                    else:
                        @block.vector
                        def _(v, k4=k4):
                            base = N["fmm"] - 8
                            for j in range(8):
                                half = j % 2
                                bank = ps[4] if half == 0 else ps[5]
                                v.wait_ge(sem("fmm"), base + j + 1)
                                v.tensor_mul(out=rv[:, j * 512:(j + 1) * 512],
                                             in0=rv[:, j * 512:(j + 1) * 512],
                                             in1=bank[:, 0:512])
                                v.scalar_tensor_tensor(
                                    out=rv[:, j * 512:(j + 1) * 512],
                                    in0=rv[:, j * 512:(j + 1) * 512],
                                    scalar=mod2[:, GB * 8 + j:GB * 8 + j + 1],
                                    in1=xmid[:, j * 513 + 1:j * 513 + 513],
                                    op0=ALU.mult, op1=ALU.add)
                                inc(v.engine_nop(), "fev", 1)

                if S4PART < 6:
                    continue
                @block.tensor
                def _(t, k4=k4):
                    W(t, "fev")
                    for tt in range(4):
                        for jj in range(8):
                            idx = tt * 8 + jj
                            if N["otr"] >= 2:
                                t.wait_ge(sem("oev"), N["otr"] - 1)
                            bank = ps[2 + (N["otr"] % 2)]
                            inc(t.transpose(
                                bank[:, 0:128],
                                rv[:, jj * 512 + tt * 128:jj * 512 + (tt + 1) * 128],
                                eye[:, :]), "otr", 1)

                @block.scalar
                def _(a, k4=k4):
                    base = N["otr"] - 32
                    for tt in range(4):
                        half = tt % 2
                        if 4 * k4 + tt >= 2:
                            a.wait_ge(sem("ost"), 16 * (4 * k4 + tt - 1))
                        for jj in range(8):
                            idx = tt * 8 + jj
                            a.wait_ge(sem("otr"), base + idx + 1)
                            bank = ps[2 + ((base + idx) % 2)]
                            inc(a.activation(
                                out=h2[:, half * DM + jj * 128:
                                       half * DM + (jj + 1) * 128],
                                in_=bank[:, 0:128],
                                func=AF.Copy), "oev", 1)

                @block.sync
                def _(s, k4=k4, P0=P0):
                    for tt in range(4):
                        half = tt % 2
                        s.wait_ge(sem("oev"), N["oev"] - 8 * (3 - tt))
                        inc(s.dma_start(
                            out=out_q[P0 + tt * 128:P0 + (tt + 1) * 128, :],
                            in_=h2[:, half * DM:(half + 1) * DM]), "ost", 16)

            NOST = N["ost"] if S4PART >= 6 else 0

        @block.sync
        def _(s):
            if S4PART >= 6:
                s.wait_ge(sem("ost"), NOST)
            else:
                for tg in ("yev4", "xev4", "gat", "woev", "h2d", "fev"):
                    if tg in k.n and k.n[tg] > 0:
                        W(s, tg)
            if debug:
                for tg in ("dbgX", "dbgS"):
                    if tg in k.n:
                        s.wait_ge(sem(tg), N[tg])

    return nc



def _prep_inputs(inputs):
    x = np.ascontiguousarray(inputs["L0_attn_x"], np.float32)
    W_in = np.ascontiguousarray(inputs["L0_attn_W_in"], np.float32)
    conv_w = np.asarray(inputs["L0_attn_conv_w"], np.float32)
    conv_b = np.asarray(inputs["L0_attn_conv_b"], np.float32)
    W_x = np.ascontiguousarray(inputs["L0_attn_W_x"], np.float32)
    A_log = np.asarray(inputs["L0_attn_A_log"], np.float32)
    dt_bias = np.asarray(inputs["L0_attn_dt_bias"], np.float32)
    D = np.asarray(inputs["L0_attn_D"], np.float32)
    W_out = np.ascontiguousarray(inputs["L0_attn_W_out"], np.float32)
    W_r = np.ascontiguousarray(inputs["L0_ffn_W_r"], np.float32)
    W_k = np.ascontiguousarray(inputs["L0_ffn_W_k"], np.float32)
    W_v = np.ascontiguousarray(inputs["L0_ffn_W_v"], np.float32)
    mu_r = np.asarray(inputs["L0_ffn_mu_r"], np.float32)
    mu_k = np.asarray(inputs["L0_ffn_mu_k"], np.float32)
    c = np.asarray(inputs["L0_c"], np.float32)
    wmod_ = np.ascontiguousarray(inputs["L0_wmod"], np.float32)

    def pack16(vec):
        return np.ascontiguousarray(vec.reshape(16, 128).T)

    def pack8(vec):
        return np.ascontiguousarray(vec.reshape(8, 128).T)

    convw_p = np.ascontiguousarray(
        conv_w.reshape(16, 128, 4).transpose(1, 0, 2).reshape(128, 64))
    convb_p = pack16(conv_b)
    beta_p = pack16(dt_bias)
    dvec_p = pack16(D)
    mur_p = pack8(mu_r)
    muk_p = pack8(mu_k)
    eye_ = np.eye(128, dtype=np.float32)
    onesr_ = np.ones((1, 128), np.float32)
    onesc_ = np.ones((128, 1), np.float32)

    in_maps = []
    for r in range(8):
        b, q = r // 4, r % 4
        nsl = slice(16 * q, 16 * q + 16)
        wx_sl = np.ascontiguousarray(np.concatenate(
            [W_x[:, 0:64][:, nsl], W_x[:, 64:128][:, nsl], W_x[:, 128:129]],
            axis=1))
        alog_sl = A_log[:, nsl]
        alog_p = np.ascontiguousarray(
            alog_sl.reshape(16, 128, 16).transpose(1, 0, 2).reshape(128, 256))
        xq = np.zeros((XQR, DM), np.float32)
        lo = q * QT - 5
        src_lo = max(0, lo)
        xq[src_lo - lo:, :] = x[b, src_lo:(q + 1) * QT, :]
        halo = np.full((128, 1), 0.0 if q == 0 else 1.0, np.float32)
        in_maps.append({
            "x_b": x[b], "x_q": xq, "w_in": W_in, "wx_p": wx_sl,
            "w_out": W_out, "w_r": W_r, "w_k": W_k, "w_v": W_v,
            "wmod": wmod_, "convw_p": convw_p, "convb_p": convb_p,
            "beta_p": beta_p, "dvec_p": dvec_p, "alog_p": alog_p,
            "mur_p": mur_p, "muk_p": muk_p,
            "c_p": pack8(c[b]).copy(), "eye_in": eye_, "onesr_in": onesr_,
            "onesc_in": onesc_, "halo_in": halo,
        })
    return in_maps


_NC_CACHE = {}


def kernel(**inputs):
    in_maps = _prep_inputs(inputs)
    key = ("prog", DEBUG, STAGES, S1CH, S1PART, S4PART, S4HALO, S4TILES, S4SUB)
    if key not in _NC_CACHE:
        _NC_CACHE[key] = build_program(debug=DEBUG, stages=STAGES)
    nc = _NC_CACHE[key]
    res = run_bass_kernel_spmd(nc, in_maps, core_ids=list(range(8)))
    out = np.zeros((2, T, DM), np.float32)
    for r in range(8):
        b, q = r // 4, r % 4
        if "out_q" in res.results[r]:
            out[b, q * QT:(q + 1) * QT, :] = res.results[r]["out_q"]
    if DEBUG:
        kernel._dbg = res.results
    return out
